# version 17
# baseline (speedup 1.0000x reference)
import numpy as np
import concourse.bass as bass
import concourse.mybir as mybir
from concourse.bass_utils import run_bass_kernel_spmd

F32 = mybir.dt.float32
BF16 = mybir.dt.bfloat16
AF = mybir.ActivationFunctionType
OP = mybir.AluOpType
AX = mybir.AxisListType

ENGINES = ("sync", "scalar", "vector", "tensor", "gpsimd")


DMA_SLOTS = 8


class _Op:
    __slots__ = ("idx", "engine", "fn", "reads", "writes", "is_dma",
                 "token_needed", "deps", "tag", "sem_name")

    def __init__(self, idx, engine, fn, reads, writes, is_dma, tag=""):
        self.idx = idx
        self.engine = engine
        self.fn = fn
        self.reads = tuple(reads)
        self.writes = tuple(writes)
        self.is_dma = is_dma
        self.token_needed = False
        self.deps = []
        self.tag = tag
        self.sem_name = engine


class _Prog:

    def __init__(self):
        self.ops = []
        self.last_writer = {}
        self.readers = {}
        self.last_on = {}
        self.dma_n = {}
        self.slot_last = {}

    def add(self, engine, fn, reads=(), writes=(), is_dma=False, tag=""):
        op = _Op(len(self.ops), engine, fn, reads, writes, is_dma, tag)
        deps = {}
        for k in op.reads:
            w = self.last_writer.get(k)
            if w is not None:
                deps[w.idx] = w
            if k.startswith("pb") and engine in ("scalar", "vector"):
                for r in self.readers.get(k, ()):
                    if r.engine in ("scalar", "vector") and \
                            r.engine != engine:
                        deps[r.idx] = r
            self.readers.setdefault(k, []).append(op)
        for k in op.writes:
            w = self.last_writer.get(k)
            if w is not None:
                deps[w.idx] = w
            for r in self.readers.get(k, ()):
                if r.engine != op.engine:
                    deps[r.idx] = r
        for k in op.writes:
            self.last_writer[k] = op
            self.readers[k] = []
        if is_dma:
            n = self.dma_n.get(engine, 0)
            self.dma_n[engine] = n + 1
            slot = n % DMA_SLOTS
            op.sem_name = f"{engine}_d{slot}"
            prev = self.slot_last.get(op.sem_name)
            if prev is not None:
                deps[prev.idx] = prev
            self.slot_last[op.sem_name] = op
        deps.pop(op.idx, None)
        op.deps = list(deps.values())
        self.ops.append(op)
        self.last_on[engine] = op
        return op

    def barrier(self):
        lasts = list(self.last_on.values()) + list(self.slot_last.values())
        for e in ENGINES:
            deps = [o for o in lasts if o is not self.last_on.get(e)]
            if not deps:
                continue
            op = _Op(len(self.ops), e, lambda eng: None, (), (), False,
                     tag="barrier")
            op.deps = deps
            self.ops.append(op)
            self.last_on[e] = op
        self.last_writer.clear()
        self.readers.clear()

    def sem_names(self):
        names = set(ENGINES)
        for q in ("sync", "gpsimd", "scalar", "vector"):
            for s in range(DMA_SLOTS):
                names.add(f"{q}_d{s}")
        return sorted(names)

    def emit(self, block, sems):
        for op in self.ops:
            if op.is_dma:
                op.token_needed = True
            for d in op.deps:
                d.token_needed = True
        counts = {}
        tokval = {}
        for op in self.ops:
            if op.token_needed:
                c = counts.get(op.sem_name, 0) + (16 if op.is_dma else 1)
                counts[op.sem_name] = c
                tokval[op.idx] = c
        per_engine = {e: [] for e in ENGINES}
        for op in self.ops:
            per_engine[op.engine].append(op)

        def mk(ename):
            ops_list = per_engine[ename]

            def closure(eng):
                waited = {}
                for op in ops_list:
                    for d in op.deps:
                        v = tokval[d.idx]
                        if waited.get(d.sem_name, -1) < v:
                            eng.wait_ge(sems[d.sem_name], v)
                            waited[d.sem_name] = v
                    inst = op.fn(eng)
                    if op.token_needed:
                        inst.then_inc(sems[op.sem_name],
                                      16 if op.is_dma else 1)
            return closure

        for e in ENGINES:
            if per_engine[e]:
                getattr(block, e)(mk(e))


def _build_nc(stop_after=None):
    B, S, D, H, HD = 2, 2048, 1024, 4, 256
    NT = S // 128
    NCH = D // 128
    INTER = 4096

    nc = bass.Bass("TRN2", target_bir_lowering=False)

    def dram(n, sh, kind=None, dt=F32):
        t = (nc.dram_tensor(n, sh, dt) if kind is None
             else nc.dram_tensor(n, sh, dt, kind=kind))
        return t.ap()

    x_in = dram("x", [B, S, D], "ExternalInput")
    wqt = dram("wqt", [D, D], "ExternalInput")
    wkt = dram("wkt", [D, D], "ExternalInput")
    wvt = dram("wvt", [D, D], "ExternalInput")
    wot = dram("wot", [D, D], "ExternalInput")
    bo_in = dram("bo", [D], "ExternalInput")
    beta_in = dram("betas", [1, H], "ExternalInput")
    memz_in = dram("memz", [H, HD, HD + 1], "ExternalInput")
    w1_in = dram("w1", [INTER, D], "ExternalInput")
    w2t_in = dram("w2t", [INTER, D], "ExternalInput")
    b1_in = dram("b1", [INTER], "ExternalInput")
    b2_in = dram("b2", [D], "ExternalInput")
    idn_in = dram("idn", [128, 128], "ExternalInput")
    dmask_in = dram("dmask", [128, 128], "ExternalInput")
    y_out = dram("y", [B, S, D], "ExternalOutput")

    qT_s = dram("qT_s", [2, NCH, 128, S], dt=BF16)
    kT_s = dram("kT_s", [2, NCH, 128, S], dt=BF16)
    v_s = dram("v_s", [NT, 128, D])
    am_s = dram("am_s", [H, NT, 128, HD])
    ln1_s = dram("ln1_s", [NT, 128, D])
    mixT_s = dram("mixT_s", [2, NCH, 128, S], dt=BF16)
    be_s = dram("be_s", [1, D])
    wq_hl = dram("wq_hl", [2, NCH, 128, D], dt=BF16)
    wk_hl = dram("wk_hl", [2, NCH, 128, D], dt=BF16)
    wv_hl = dram("wv_hl", [2, NCH, 128, D], dt=BF16)
    wo_hl = dram("wo_hl", [2, NCH, 128, D], dt=BF16)

    x_r = x_in.rearrange("b (t p) d -> b t p d", p=128)
    y_r = y_out.rearrange("b (t p) d -> b t p d", p=128)
    wqt_r = wqt.rearrange("(c p) o -> c p o", p=128)
    wkt_r = wkt.rearrange("(c p) o -> c p o", p=128)
    wvt_r = wvt.rearrange("(c p) o -> c p o", p=128)
    wot_r = wot.rearrange("(c p) o -> c p o", p=128)
    bo_r = bo_in.rearrange("(c p) -> p c", p=128)
    b2_r = b2_in.rearrange("(c p) -> p c", p=128)
    b1_r = b1_in.rearrange("(c p) -> p c", p=128)
    be_r = be_s.rearrange("a (c p) -> (a p) c", p=128)
    memz_r = memz_in.rearrange("h (c p) e -> h c p e", p=128)
    w1_r = w1_in.rearrange("(c p) d -> c p d", p=128)
    w2t_r = w2t_in.rearrange("(c p) d -> c p d", p=128)

    from contextlib import ExitStack
    es = ExitStack()
    sb = lambda n, w: es.enter_context(nc.sbuf_tensor(n, [128, w], F32))
    mT = sb("mT", NCH * D)
    vhb = sb("vhb", NT * (HD + 1))

    class _Split:
        def __init__(self, parts, w=4096):
            self.parts = parts
            self.w = w

        def __getitem__(self, key):
            ps, cs = key
            lo, hi = cs.start, cs.stop
            i = lo // self.w
            assert hi <= (i + 1) * self.w, (lo, hi)
            return self.parts[i][ps, lo - i * self.w:hi - i * self.w]

    cA = _Split([sb(f"cA{i}", 4096) for i in range(4)])
    cB = _Split([sb(f"cB{i}", 4096) for i in range(3)])
    cC = _Split([sb(f"cC{i}", 4096) for i in range(2)])
    idn = sb("idn_t", 128)
    dmask = sb("dmask_t", 128)
    memz = sb("memz_t", 2 * H * (HD + 1))
    boT = sb("boT", NCH)
    beT = sb("beT", NCH)
    b2T = sb("b2T", NCH)
    b1s = sb("b1s", INTER // 128)
    gt = sb("gt", 2 * H)
    sA = sb("sA", 64)
    s2 = sb("s2", 32)
    sp = sb("sp", 64)
    trash = sb("trash", 512)
    pb = [es.enter_context(nc.psum_tensor(f"pb{i}", [128, 512], F32))
          for i in range(8)]

    mTh = mT[:, 0:4096].bitcast(BF16)
    mTl = mT[:, 4096:8192].bitcast(BF16)
    lnTh = cA[:, 0:4096].bitcast(BF16)
    lnTl = cA[:, 4096:8192].bitcast(BF16)
    wth = cA[:, 8192:12288].bitcast(BF16)
    wtl = cA[:, 12288:16384].bitcast(BF16)
    def expS(lo, hi):
        return cA[:, lo:hi]
    qhh = cA[:, 8192:10240].bitcast(BF16)
    qhl = cA[:, 10240:12288].bitcast(BF16)
    khh = cA[:, 12288:14336].bitcast(BF16)
    khl = cA[:, 14336:16384].bitcast(BF16)
    woh = cA[:, 8192:12288].bitcast(BF16)
    wol = cA[:, 12288:16384].bitcast(BF16)
    ones1 = cB[0:1, 1028:1156]
    brow = cB[0:1, 0:1024]
    betar = cB[0:1, 1024:1028]

    P = _Prog()
    sems = {}
    for name in P.sem_names():
        sems[name] = es.enter_context(nc.semaphore(f"s_{name}"))
    A = P.add

    A("sync", lambda e: e.dma_start(out=idn[:], in_=idn_in[:]),
      writes=["idn"], is_dma=True, tag="ld_idn")
    A("sync", lambda e: e.dma_start(out=dmask[:], in_=dmask_in[:]),
      writes=["dmask"], is_dma=True, tag="ld_dmask")
    for h in range(H):
        for c in range(2):
            A("sync", lambda e, h=h, c=c: e.dma_start(
                out=memz[:, (2 * h + c) * (HD + 1):(2 * h + c + 1) * (HD + 1)],
                in_=memz_r[h, c]),
              writes=["memz"], is_dma=True, tag="ld_memz")
    A("sync", lambda e: e.dma_start(allow_slow_non_contiguous=True,
                                    out=boT[:], in_=bo_r[:]),
      writes=["boT"], is_dma=True)
    A("sync", lambda e: e.dma_start(allow_slow_non_contiguous=True,
                                    out=b2T[:], in_=b2_r[:]),
      writes=["b2T"], is_dma=True)
    A("sync", lambda e: e.dma_start(allow_slow_non_contiguous=True,
                                    out=b1s[:], in_=b1_r[:]),
      writes=["b1s"], is_dma=True)
    A("sync", lambda e: e.dma_start(out=betar[:], in_=beta_in[:]),
      writes=["betar"], is_dma=True)
    for i in range(NT):
        A("gpsimd", lambda e, i=i: e.memset(
            vhb[:, i * (HD + 1) + HD:(i + 1) * (HD + 1)], 1.0),
          writes=[f"vones{i}"])
    A("gpsimd", lambda e: e.memset(ones1[:], 1.0), writes=["ones1"])

    A("scalar", lambda e: e.activation(betar[:], betar[:], AF.Sigmoid),
      reads=["betar"], writes=["betar"])
    A("tensor", lambda e: e.matmul(pb[0][:, 0:H], ones1[:], betar[:],
                                   start=True, stop=True),
      reads=["betar", "ones1"], writes=["pb0"], tag="gbc")
    A("vector", lambda e: e.tensor_copy(gt[:, 0:H], pb[0][:, 0:H]),
      reads=["pb0"], writes=["gt"])
    A("vector", lambda e: e.tensor_scalar(gt[:, H:2 * H], gt[:, 0:H],
                                          -1.0, 1.0, op0=OP.mult, op1=OP.add),
      reads=["gt"], writes=["gt1m"])

    NI = INTER // 128
    wsin = [cC[:, 5120:6144], cC[:, 5120:6144]]
    wsh = cC[:, 7168:7680].bitcast(BF16)
    wsl = cC[:, 7680:8192].bitcast(BF16)

    def _wsplit_steps():
        for wi, (wt_r, whl) in enumerate([(wqt_r, wq_hl), (wkt_r, wk_hl),
                                          (wvt_r, wv_hl), (wot_r, wo_hl)]):
            for c in range(NCH):
                yield (wi, c, wt_r, whl)

    def _emit_wsplit(step):
        wi, c, wt_r, whl = step
        k = 0
        buf = wsin[k]
        A("sync", lambda e, buf=buf, wt_r=wt_r, c=c: e.dma_start(
            out=buf, in_=wt_r[c]),
          writes=[f"wsin{k}"], is_dma=True, tag=f"ldws{wi}.{c}")
        A("scalar", lambda e, buf=buf: e.copy(wsh[:], buf),
          reads=[f"wsin{k}"], writes=["wsh"], tag="wsplit_h")
        A("vector", lambda e, buf=buf: e.tensor_tensor(
            wsl[:], buf, wsh[:], op=OP.subtract),
          reads=[f"wsin{k}", "wsh"], writes=["wsl"], tag="wsplit_l")
        A("gpsimd", lambda e, whl=whl, c=c: e.dma_start(
            out=whl[0, c], in_=wsh[:]),
          reads=["wsh"], writes=["whld"], is_dma=True)
        A("gpsimd", lambda e, whl=whl, c=c: e.dma_start(
            out=whl[1, c], in_=wsl[:]),
          reads=["wsl"], writes=["whld"], is_dma=True)

    _wsteps = iter(_wsplit_steps())
    for c in range(NI):
        buf = cC[:, (c % 2) * 1024:(c % 2) * 1024 + 1024]
        A("sync", lambda e, c=c, buf=buf: e.dma_start(out=buf, in_=w2t_r[c]),
          writes=[f"w2tc{c % 2}"], is_dma=True, tag=f"ld_w2t{c}")
        for half in range(2):
            A("tensor", lambda e, c=c, buf=buf, half=half: e.matmul(
                pb[6 + half][0:1, 0:512], b1s[:, c:c + 1],
                buf[:, half * 512:half * 512 + 512],
                start=(c == 0), stop=(c == NI - 1)),
              reads=[f"w2tc{c % 2}", "b1s"], writes=[f"pb{6 + half}"],
              tag=f"be{c}.{half}")
    for half in range(2):
        A("vector", lambda e, half=half: e.tensor_copy(
            brow[:, half * 512:half * 512 + 512], pb[6 + half][0:1, 0:512]),
          reads=[f"pb{6 + half}"], writes=["brow"])
    A("gpsimd", lambda e: e.dma_start(out=be_s[:], in_=brow[:]),
      reads=["brow"], writes=["be_d"], is_dma=True)
    A("sync", lambda e: e.dma_start(allow_slow_non_contiguous=True,
                                    out=beT[:], in_=be_r[:]),
      reads=["be_d"], writes=["beT"], is_dma=True)
    A("vector", lambda e: e.tensor_tensor(beT[:], beT[:], b2T[:], op=OP.add),
      reads=["beT", "b2T"], writes=["beT"])

    w1hb = [cC[:, 0 + i * 512:512 + i * 512].bitcast(BF16) for i in range(2)]
    w1lb = [cC[:, 1024 + i * 512:1536 + i * 512].bitcast(BF16)
            for i in range(2)]
    w2hb = [cC[:, 6144 + i * 256:6400 + i * 256].bitcast(BF16)
            for i in range(2)]
    w2lb = [cC[:, 6656 + i * 256:6912 + i * 256].bitcast(BF16)
            for i in range(2)]
    for osl in range(2):
        for c in range(NI):
            s = c % 2
            w1b = cC[:, 2048 + s * 1024:2048 + s * 1024 + 1024]
            w2b = cC[:, 4096 + s * 512:4096 + s * 512 + 512]
            A("sync", lambda e, c=c, w1b=w1b: e.dma_start(out=w1b,
                                                          in_=w1_r[c]),
              writes=[f"w1c{s}", "w2tc0" if s == 0 else "w2tc1"],
              is_dma=True, tag=f"ld_w1_{osl}_{c}")
            A("sync", lambda e, c=c, w2b=w2b, osl=osl: e.dma_start(
                out=w2b, in_=w2t_r[c][:, osl * 512:osl * 512 + 512]),
              writes=[f"w2c{s}"], is_dma=True)
            A("scalar", lambda e, s=s, w1b=w1b: e.copy(w1hb[s][:], w1b),
              reads=[f"w1c{s}"], writes=[f"w1h{s}", "w2tc0" if s == 0
                                         else "w2tc1"])
            A("vector", lambda e, s=s, w1b=w1b: e.tensor_tensor(
                w1lb[s][:], w1b, w1hb[s][:], op=OP.subtract),
              reads=[f"w1c{s}", f"w1h{s}"], writes=[f"w1l{s}"])
            A("scalar", lambda e, s=s, w2b=w2b: e.copy(w2hb[s][:], w2b),
              reads=[f"w2c{s}"], writes=[f"w2h{s}"])
            A("vector", lambda e, s=s, w2b=w2b: e.tensor_tensor(
                w2lb[s][:], w2b, w2hb[s][:], op=OP.subtract),
              reads=[f"w2c{s}", f"w2h{s}"], writes=[f"w2l{s}"])

            def mmg(e, c=c, s=s):
                n = 0
                for p in range(3):
                    a_ = w1hb[s] if p < 2 else w1lb[s]
                    b_ = w2hb[s] if p != 1 else w2lb[s]
                    for it in range(8):
                        inst = e.matmul(pb[it][:, 0:512],
                                        a_[:, it * 128:it * 128 + 128], b_[:],
                                        start=(c == 0 and p == 0),
                                        stop=(c == NI - 1 and p == 2))
                        n += 1
                return inst
            A("tensor", mmg,
              reads=[f"w1h{s}", f"w1l{s}", f"w2h{s}", f"w2l{s}"],
              writes=[f"pb{it}" for it in range(8)], tag=f"mT{osl}.{c}")
            ws = next(_wsteps, None)
            if ws is not None:
                _emit_wsplit(ws)
        for it in range(8):
            A("scalar", lambda e, it=it, osl=osl: e.copy(
                mTh[:, it * D + osl * 512:it * D + osl * 512 + 512],
                pb[it][:, 0:512]),
              reads=[f"pb{it}"], writes=[f"mTh{it}"], tag=f"evmTh{osl}.{it}")
            A("vector", lambda e, it=it, osl=osl: e.tensor_tensor(
                mTl[:, it * D + osl * 512:it * D + osl * 512 + 512],
                pb[it][:, 0:512],
                mTh[:, it * D + osl * 512:it * D + osl * 512 + 512],
                op=OP.subtract),
              reads=[f"pb{it}", f"mTh{it}"], writes=[f"mTl{it}"],
              tag=f"evmTl{osl}.{it}")
    for ws in _wsteps:
        _emit_wsplit(ws)
    for i in range(2):
        A("sync", lambda e, i=i: e.dma_start(
            out=cB[:, 6144 + i * 1024:6144 + (i + 1) * 1024], in_=x_r[0, i]),
          writes=[f"xs{i}"], is_dma=True, tag=f"hoistx0.{i}")
    for c in range(NCH):
        A("sync", lambda e, c=c: e.dma_start(
            out=wth[:, c * 1024:(c + 1) * 1024], in_=wq_hl[0, c]),
          writes=[f"wth{c}"], is_dma=True, tag=f"hoistwq0.{c}")
        A("sync", lambda e, c=c: e.dma_start(
            out=wtl[:, c * 1024:(c + 1) * 1024], in_=wq_hl[1, c]),
          writes=[f"wtl{c}"], is_dma=True)
    P.barrier()

    def mm3(pq, pairs, n_free=512):
        def fn(e):
            n = len(pairs) * 3
            i = 0
            for p in range(3):
                for (lh, ll, rh, rl) in pairs:
                    a = lh if p < 2 else ll
                    bb = rh if p != 1 else rl
                    inst = e.matmul(pq[:, 0:n_free], a, bb,
                                    start=(i == 0), stop=(i == n - 1))
                    i += 1
            return inst
        return fn

    eps = 1e-5
    if stop_after == "p0":
        with nc.Block() as block:
            P.emit(block, sems)
        es.close()
        return nc
    for b in range(B):
        for hf in range(2):
            xst = [cB[:, 6144 + i * 1024:6144 + (i + 1) * 1024]
                   for i in range(2)]
            for tt in range(8):
                ti = hf * 8 + tt
                xb = xst[tt % 2]
                if not (hf == 0 and tt < 2):
                    A("sync", lambda e, b=b, ti=ti, xb=xb: e.dma_start(
                        out=xb, in_=x_r[b, ti]),
                      writes=[f"xs{tt % 2}"], is_dma=True, tag=f"ldx{b}.{ti}")
                A("vector", lambda e, xb=xb, ti=ti: e.reduce_sum(
                    sA[:, ti:ti + 1], xb, axis=AX.X),
                  reads=[f"xs{tt % 2}"], writes=[f"sum{ti}"])
                A("scalar", lambda e, xb=xb, ti=ti: e.activation(
                    trash[:], xb[:, 0:512], AF.Square,
                    accum_out=sA[:, 16 + ti:17 + ti]),
                  reads=[f"xs{tt % 2}"], writes=[f"ssq{ti}", "trash"])
                A("scalar", lambda e, xb=xb, tt=tt: e.activation(
                    trash[:], xb[:, 512:1024], AF.Square,
                    accum_out=s2[:, 16 + tt:17 + tt]),
                  reads=[f"xs{tt % 2}"], writes=[f"ssqh{tt}", "trash"])
            sl8 = slice(hf * 8, hf * 8 + 8)
            A("vector", lambda e, sl8=sl8, hf=hf: e.tensor_scalar(
                sA[:, 32 + hf * 8:40 + hf * 8], sA[:, sl8], 1.0 / D, None,
                op0=OP.mult),
              reads=[f"sum{hf * 8 + i}" for i in range(8)],
              writes=["mean8"], tag="mean")
            A("vector", lambda e, hf=hf: e.tensor_tensor(
                sA[:, 16 + hf * 8:24 + hf * 8], sA[:, 16 + hf * 8:24 + hf * 8],
                s2[:, 16:24], op=OP.add),
              reads=[f"ssq{hf * 8 + i}" for i in range(8)] +
                    [f"ssqh{i}" for i in range(8)],
              writes=["ssqtot"])
            A("vector", lambda e, hf=hf: e.tensor_scalar(
                sA[:, 16 + hf * 8:24 + hf * 8], sA[:, 16 + hf * 8:24 + hf * 8],
                1.0 / D, None, op0=OP.mult),
              reads=["ssqtot"], writes=["ex28"])
            A("vector", lambda e, hf=hf: e.tensor_tensor(
                s2[:, 24:32], sA[:, 32 + hf * 8:40 + hf * 8],
                sA[:, 32 + hf * 8:40 + hf * 8], op=OP.mult),
              reads=["mean8"], writes=["msq8"])
            A("vector", lambda e, hf=hf: e.scalar_tensor_tensor(
                s2[:, 24:32], sA[:, 16 + hf * 8:24 + hf * 8], eps,
                s2[:, 24:32], op0=OP.add, op1=OP.subtract),
              reads=["ex28", "msq8"], writes=["veps8"])
            A("scalar", lambda e, hf=hf: e.activation(
                sA[:, 48 + hf * 8:56 + hf * 8], s2[:, 24:32], AF.Sqrt),
              reads=["veps8"], writes=["sd8"], tag="sd")
            A("vector", lambda e, hf=hf: e.reciprocal(
                sA[:, 16 + hf * 8:24 + hf * 8], sA[:, 48 + hf * 8:56 + hf * 8]),
              reads=["sd8"], writes=["rstd8"])
            for tt in range(8):
                ti = hf * 8 + tt
                xb = xst[tt % 2]
                lnb = cB[:, 8192 + (tt % 2) * 1024:8192 + (tt % 2 + 1) * 1024]
                A("sync", lambda e, b=b, ti=ti, xb=xb: e.dma_start(
                    out=xb, in_=x_r[b, ti]),
                  writes=[f"xs{tt % 2}"], is_dma=True)
                A("vector", lambda e, xb=xb, lnb=lnb, ti=ti: e.tensor_scalar(
                    lnb, xb, sA[:, 32 + ti:33 + ti], sA[:, 16 + ti:17 + ti],
                    op0=OP.subtract, op1=OP.mult),
                  reads=[f"xs{tt % 2}", "mean8", "rstd8"],
                  writes=[f"ln1t{tt % 2}"], tag=f"ln1_{b}.{ti}")
                A("gpsimd", lambda e, lnb=lnb, ti=ti: e.dma_start(
                    out=ln1_s[ti], in_=lnb),
                  reads=[f"ln1t{tt % 2}"], writes=[f"ln1d{ti}"], is_dma=True)
                for c in range(NCH):
                    pbk = c % 4
                    dst = tt * 128
                    A("tensor", lambda e, lnb=lnb, c=c, pbk=pbk: e.transpose(
                        pb[pbk][:, 0:128], lnb[:, c * 128:c * 128 + 128],
                        idn[:]),
                      reads=[f"ln1t{tt % 2}", "idn"], writes=[f"pb{pbk}"],
                      tag=f"trln{ti}.{c}")
                    hg = tt // 4
                    A("scalar", lambda e, c=c, dst=dst, pbk=pbk: e.copy(
                        lnTh[:, c * 1024 + dst:c * 1024 + dst + 128],
                        pb[pbk][:, 0:128]),
                      reads=[f"pb{pbk}"], writes=[f"lnTh{c}t{hg}"])
                    A("vector", lambda e, c=c, dst=dst, pbk=pbk:
                      e.tensor_tensor(
                          lnTl[:, c * 1024 + dst:c * 1024 + dst + 128],
                          pb[pbk][:, 0:128],
                          lnTh[:, c * 1024 + dst:c * 1024 + dst + 128],
                          op=OP.subtract),
                      reads=[f"pb{pbk}", f"lnTh{c}t{hg}"],
                      writes=[f"lnTl{c}t{hg}"])
            def lnT_keys_for(ts2):
                return [f"lnTh{c}t{ts2}" for c in range(NCH)] + \
                       [f"lnTl{c}t{ts2}" for c in range(NCH)]
            lnT_keys = lnT_keys_for(0) + lnT_keys_for(1)
            qk_h = [cB[:, 3072 + i * 256:3072 + i * 256 + 256].bitcast(BF16)
                    for i in range(2)]
            qk_l = [cB[:, 3584 + i * 256:3584 + i * 256 + 256].bitcast(BF16)
                    for i in range(2)]
            if stop_after == "A1":
                P.barrier()
                break
            if stop_after in ("A1w", "A1m", "A1e", "A1f", "A1g", "A1h"):
                for c in range(NCH):
                    A("sync", lambda e, c=c: e.dma_start(
                        out=wth[:, c * 1024:(c + 1) * 1024], in_=wq_hl[0, c]),
                      writes=[f"wth{c}"], is_dma=True)
                    A("sync", lambda e, c=c: e.dma_start(
                        out=wtl[:, c * 1024:(c + 1) * 1024], in_=wq_hl[1, c]),
                      writes=[f"wtl{c}"], is_dma=True)
                if stop_after in ("A1m", "A1e"):
                    wk = [f"wth{c}" for c in range(NCH)] + \
                         [f"wtl{c}" for c in range(NCH)]
                    lk = [f"lnTh{c}" for c in range(NCH)] + \
                         [f"lnTl{c}" for c in range(NCH)]
                    for h in range(H):
                        for ts_ in range(2):
                            for c01 in range(2):
                                ot = 2 * h + c01
                                pq = pb[4 + c01]
                                pqk = f"pb{4 + c01}"
                                pairs = [(wth[:, c * 1024 + ot * 128:
                                              c * 1024 + ot * 128 + 128],
                                          wtl[:, c * 1024 + ot * 128:
                                              c * 1024 + ot * 128 + 128],
                                          lnTh[:, c * 1024 + ts_ * 512:
                                               c * 1024 + ts_ * 512 + 512],
                                          lnTl[:, c * 1024 + ts_ * 512:
                                               c * 1024 + ts_ * 512 + 512])
                                         for c in range(NCH)]
                                A("tensor", mm3(pq, pairs), reads=wk + lk,
                                  writes=[pqk], tag=f"qm{ot}.{ts_}")
                                ev = cB[:, 2048 + c01 * 512:
                                        2048 + c01 * 512 + 512]
                                A("vector", lambda e, ev=ev, pq=pq:
                                  e.tensor_copy(ev, pq[:, 0:512]),
                                  reads=[pqk], writes=[f"ev{c01}"])
                                if stop_after in ("A1e", "A1f", "A1g", "A1h"):
                                    A("vector", lambda e, pq=pq, c01=c01:
                                      e.tensor_copy(qk_h[c01][:], pq[:, 0:512]),
                                      reads=[pqk], writes=[f"qkh{c01}"])
                                    A("vector", lambda e, pq=pq, c01=c01:
                                      e.tensor_tensor(qk_l[c01][:],
                                                      pq[:, 0:512],
                                                      qk_h[c01][:],
                                                      op=OP.subtract),
                                      reads=[pqk, f"qkl{c01}"],
                                      writes=[f"qkl{c01}"])
                                if stop_after in ("A1f", "A1g", "A1h"):
                                    mb = cB[:, 1024 + c01 * 512:
                                            1024 + c01 * 512 + 512]
                                    mk = f"qm{c01}"
                                    A("vector", lambda e, ev=ev, mb=mb:
                                      e.tensor_scalar(mb, ev, 0.0, None,
                                                      op0=OP.min),
                                      reads=[f"ev{c01}"], writes=[mk])
                                    A("scalar", lambda e, mb=mb: e.activation(
                                        mb, mb, AF.Exp),
                                      reads=[mk], writes=[mk])
                                    sqb = cB[:, c01 * 512:c01 * 512 + 512]
                                    A("vector", lambda e, ev=ev, mb=mb,
                                      sqb=sqb: e.scalar_tensor_tensor(
                                          sqb, ev, 0.0, mb, op0=OP.max,
                                          op1=OP.add),
                                      reads=[f"ev{c01}", mk],
                                      writes=[f"sq{c01}"])
                            if stop_after in ("A1g", "A1h"):
                                for jj in range(4):
                                    pa = pb[6 + jj % 2]
                                    pak = f"pb{6 + jj % 2}"

                                    def mma2(e, h=h, jj=jj, pa=pa):
                                        for c in range(2):
                                            inst = e.matmul(
                                                pa[:, 0:HD + 1],
                                                cB[:, c * 512 + jj * 128:
                                                    c * 512 + jj * 128 + 128],
                                                memz[:, (2 * h + c) * (HD + 1):
                                                     (2 * h + c + 1) * (HD + 1)],
                                                start=(c == 0), stop=(c == 1))
                                        return inst
                                    A("tensor", mma2,
                                      reads=["sq0", "sq1", "memz"],
                                      writes=[pak])
                                    amb = cB[:, 4096 + (jj % 2) * 256:
                                             4096 + (jj % 2) * 256 + 256]
                                    A("vector", lambda e, pa=pa, amb=amb:
                                      e.tensor_scalar(amb, pa[:, 0:HD], 1.0,
                                                      None, op0=OP.mult),
                                      reads=[pak], writes=[f"amv{jj % 2}"])
                                    if stop_after in ("A1h",):
                                        ti2 = hf * 8 + ts_ * 4 + jj
                                        A("gpsimd", lambda e, amb=amb, h=h,
                                          ti2=ti2: e.dma_start(
                                              out=am_s[h, ti2], in_=amb),
                                          reads=[f"amv{jj % 2}"],
                                          writes=["amd"], is_dma=True)
                P.barrier()
                break

            def load_whl(whl, name):
                for c in range(NCH):
                    A("sync", lambda e, c=c, whl=whl: e.dma_start(
                        out=wth[:, c * 1024:(c + 1) * 1024], in_=whl[0, c]),
                      writes=[f"wth{c}"], is_dma=True, tag=f"ld{name}h{c}")
                    A("sync", lambda e, c=c, whl=whl: e.dma_start(
                        out=wtl[:, c * 1024:(c + 1) * 1024], in_=whl[1, c]),
                      writes=[f"wtl{c}"], is_dma=True, tag=f"ld{name}l{c}")
            wt_keys = [f"wth{c}" for c in range(NCH)] + \
                      [f"wtl{c}" for c in range(NCH)]

            if hf == 1:
                load_whl(wq_hl, "q")
            for h in range(H):
                for ts_ in range(2):
                    for c01 in range(2):
                        ot = 2 * h + c01
                        pq = pb[4 + c01]
                        pqk = f"pb{4 + c01}"
                        pairs = [(wth[:, c * 1024 + ot * 128:
                                      c * 1024 + ot * 128 + 128],
                                  wtl[:, c * 1024 + ot * 128:
                                      c * 1024 + ot * 128 + 128],
                                  lnTh[:, c * 1024 + ts_ * 512:
                                       c * 1024 + ts_ * 512 + 512],
                                  lnTl[:, c * 1024 + ts_ * 512:
                                       c * 1024 + ts_ * 512 + 512])
                                 for c in range(NCH)]
                        A("tensor", mm3(pq, pairs),
                          reads=wt_keys + lnT_keys_for(ts_),
                          writes=[pqk], tag=f"q{b}.{hf}.{ot}.{ts_}")
                        ev = cB[:, 2048 + c01 * 512:2048 + c01 * 512 + 512]
                        evk = f"ev{c01}"
                        A("scalar", lambda e, ev=ev, pq=pq: e.copy(
                            ev, pq[:, 0:512]),
                          reads=[pqk], writes=[evk])
                        A("vector", lambda e, pq=pq, c01=c01: e.tensor_copy(
                            qk_h[c01][:], pq[:, 0:512]),
                          reads=[pqk], writes=[f"qkh{c01}"])
                        A("vector", lambda e, pq=pq, c01=c01: e.tensor_tensor(
                            qk_l[c01][:], pq[:, 0:512], qk_h[c01][:],
                            op=OP.subtract),
                          reads=[pqk, f"qkh{c01}"], writes=[f"qkl{c01}"])
                        for hl in range(2):
                            if stop_after in ("A2a", "A2aM1"):
                                break
                            src = qk_h[c01] if hl == 0 else qk_l[c01]
                            sk = f"qk{'h' if hl == 0 else 'l'}{c01}"
                            A("gpsimd", lambda e, src=src, hl=hl, ot=ot,
                              hf=hf, ts_=ts_: e.dma_start(
                                  out=qT_s[hl, ot][:, hf * 1024 + ts_ * 512:
                                                   hf * 1024 + ts_ * 512 + 512],
                                  in_=src[:]),
                              reads=[sk], writes=["qTd"], is_dma=True)
                        mb = cB[:, 1024 + c01 * 512:1024 + c01 * 512 + 512]
                        mk = f"qm{c01}"
                        A("vector", lambda e, ev=ev, mb=mb: e.tensor_scalar(
                            mb, ev, 0.0, None, op0=OP.min),
                          reads=[evk], writes=[mk])
                        A("scalar", lambda e, mb=mb: e.activation(
                            mb, mb, AF.Exp),
                          reads=[mk], writes=[mk], tag="eluexp")
                        sqb = cB[:, c01 * 512:c01 * 512 + 512]
                        A("vector", lambda e, ev=ev, mb=mb, sqb=sqb:
                          e.scalar_tensor_tensor(sqb, ev, 0.0, mb,
                                                 op0=OP.max, op1=OP.add),
                          reads=[evk, mk], writes=[f"sq{c01}"])
                    for jj in range(4):
                        ti = hf * 8 + ts_ * 4 + jj
                        pa = pb[6 + jj % 2]
                        pak = f"pb{6 + jj % 2}"

                        def mma(e, h=h, jj=jj, pa=pa):
                            for c in range(2):
                                inst = e.matmul(
                                    pa[:, 0:HD + 1],
                                    cB[:, c * 512 + jj * 128:
                                        c * 512 + jj * 128 + 128],
                                    memz[:, (2 * h + c) * (HD + 1):
                                         (2 * h + c + 1) * (HD + 1)],
                                    start=(c == 0), stop=(c == 1))
                            return inst
                        A("tensor", mma, reads=["sq0", "sq1", "memz"],
                          writes=[pak], tag=f"am{b}.{h}.{ti}")
                        cj = 3 * (jj % 4)
                        A("vector", lambda e, pa=pa, cj=cj: e.tensor_scalar(
                            sp[:, cj:cj + 1], pa[:, HD:HD + 1], 1e-6, None,
                            op0=OP.add),
                          reads=[pak], writes=[f"rzt{jj}"])
                        A("vector", lambda e, cj=cj: e.reciprocal(
                            sp[:, cj + 1:cj + 2], sp[:, cj:cj + 1]),
                          reads=[f"rzt{jj}"], writes=[f"rz{jj}"])
                        A("vector", lambda e, cj=cj, h=h: e.tensor_scalar(
                            sp[:, cj + 2:cj + 3], sp[:, cj + 1:cj + 2],
                            gt[:, h:h + 1], None, op0=OP.mult),
                          reads=[f"rz{jj}", "gt"], writes=[f"rzg{jj}"])
                        amb = cB[:, 4096 + (jj % 2) * 256:
                                 4096 + (jj % 2) * 256 + 256]
                        A("vector", lambda e, pa=pa, amb=amb, cj=cj:
                          e.tensor_scalar(amb, pa[:, 0:HD],
                                          sp[:, cj + 2:cj + 3], None,
                                          op0=OP.mult),
                          reads=[pak, f"rzg{jj}"], writes=[f"amv{jj % 2}"])
                        A("gpsimd", lambda e, amb=amb, h=h, ti=ti:
                          e.dma_start(out=am_s[h, ti], in_=amb),
                          reads=[f"amv{jj % 2}"], writes=["amd"], is_dma=True)
            if stop_after in ("A2", "A2a", "A2aM1"):
                P.barrier()
                break
            load_whl(wk_hl, "k")
            for ot in range(NCH):
                for ts_ in range(2):
                    pq = pb[4 + (ot * 2 + ts_) % 4]
                    pqk = f"pb{4 + (ot * 2 + ts_) % 4}"
                    pairs = [(wth[:, c * 1024 + ot * 128:
                                  c * 1024 + ot * 128 + 128],
                              wtl[:, c * 1024 + ot * 128:
                                  c * 1024 + ot * 128 + 128],
                              lnTh[:, c * 1024 + ts_ * 512:
                                   c * 1024 + ts_ * 512 + 512],
                              lnTl[:, c * 1024 + ts_ * 512:
                                   c * 1024 + ts_ * 512 + 512])
                             for c in range(NCH)]
                    A("tensor", mm3(pq, pairs),
                      reads=wt_keys + lnT_keys_for(ts_),
                      writes=[pqk], tag=f"k{b}.{hf}.{ot}.{ts_}")
                    A("vector", lambda e, pq=pq, ts_=ts_: e.tensor_copy(
                        qk_h[ts_][:], pq[:, 0:512]),
                      reads=[pqk], writes=[f"qkh{ts_}"])
                    A("vector", lambda e, pq=pq, ts_=ts_: e.tensor_tensor(
                        qk_l[ts_][:], pq[:, 0:512], qk_h[ts_][:],
                        op=OP.subtract),
                      reads=[pqk, f"qkh{ts_}"], writes=[f"qkl{ts_}"])
                    for hl in range(2):
                        src = qk_h[ts_] if hl == 0 else qk_l[ts_]
                        sk = f"qk{'h' if hl == 0 else 'l'}{ts_}"
                        A("gpsimd", lambda e, src=src, hl=hl, ot=ot, hf=hf,
                          ts_=ts_: e.dma_start(
                              out=kT_s[hl, ot][:, hf * 1024 + ts_ * 512:
                                               hf * 1024 + ts_ * 512 + 512],
                              in_=src[:]),
                          reads=[sk], writes=["kTd"], is_dma=True)
            if stop_after == "A3":
                P.barrier()
                break
            load_whl(wv_hl, "v")
            for tt in range(8):
                ti = hf * 8 + tt
                for oh in range(2):
                    pq = pb[4 + (tt * 2 + oh) % 4]
                    pqk = f"pb{4 + (tt * 2 + oh) % 4}"
                    pairs = [(lnTh[:, c * 1024 + tt * 128:
                                   c * 1024 + tt * 128 + 128],
                              lnTl[:, c * 1024 + tt * 128:
                                   c * 1024 + tt * 128 + 128],
                              wth[:, c * 1024 + oh * 512:
                                  c * 1024 + oh * 512 + 512],
                              wtl[:, c * 1024 + oh * 512:
                                  c * 1024 + oh * 512 + 512])
                             for c in range(NCH)]
                    A("tensor", mm3(pq, pairs),
                      reads=wt_keys + lnT_keys_for(tt // 4),
                      writes=[pqk], tag=f"v{b}.{ti}.{oh}")
                    ev = cB[:, 2048 + oh * 512:2048 + oh * 512 + 512]
                    evk = f"ev{oh}"
                    A("scalar", lambda e, ev=ev, pq=pq: e.copy(ev, pq[:, 0:512]),
                      reads=[pqk], writes=[evk])
                    A("gpsimd", lambda e, ev=ev, ti=ti, oh=oh: e.dma_start(
                        out=v_s[ti][:, oh * 512:oh * 512 + 512], in_=ev),
                      reads=[evk], writes=["vd"], is_dma=True)
            if hf == 1:
                P.barrier()
            if stop_after and stop_after.startswith("A"):
                break

        if stop_after and stop_after.startswith("A"):
            break
        mxsh = cB[:, 5120 + 0 * 256:5120 + 512].bitcast(BF16)
        mxsl = cB[:, 5632:6144].bitcast(BF16)
        qkbufs = [
            (qhh, qhl, khh, khl),
            (cC[:, 0:2048].bitcast(BF16), cC[:, 2048:4096].bitcast(BF16),
             cC[:, 4096:6144].bitcast(BF16), cC[:, 6144:8192].bitcast(BF16)),
        ]
        for h in range(H):
            qhh_, qhl_, khh_, khl_ = qkbufs[h % 2]
            pfx = h % 2
            for c in range(2):
                for hl, (qd, kd, qb, kb) in enumerate(
                        [(qT_s, kT_s, qhh_, khh_), (qT_s, kT_s, qhl_, khl_)]):
                    A("sync", lambda e, h=h, c=c, hl=hl, qd=qd, qb=qb:
                      e.dma_start(out=qb[:, c * 2048:(c + 1) * 2048],
                                  in_=qd[hl, 2 * h + c]),
                      writes=[f"q{'h' if hl == 0 else 'l'}{c}_{pfx}"],
                      is_dma=True, tag=f"ldqh{b}.{h}")
                    A("sync", lambda e, h=h, c=c, hl=hl, kd=kd, kb=kb:
                      e.dma_start(out=kb[:, c * 2048:(c + 1) * 2048],
                                  in_=kd[hl, 2 * h + c]),
                      writes=[f"k{'h' if hl == 0 else 'l'}{c}_{pfx}"],
                      is_dma=True)
            for i in range(NT):
                A("sync", lambda e, h=h, i=i: e.dma_start(
                    out=vhb[:, i * (HD + 1):i * (HD + 1) + HD],
                    in_=v_s[i][:, h * HD:(h + 1) * HD]),
                  writes=[f"vh{i}"], is_dma=True)
            qk_keys = [f"{x}{c}_{pfx}" for x in ("qh", "ql", "kh", "kl")
                       for c in range(2)]
            if h == 0:
                pending_trmix = []
            while pending_trmix:
                pending_trmix.pop(0)()
            for J in reversed(range(4)):
                tmax = 4 * J + 4
                def _emit_score(i):
                    pass
                for i in range(tmax):
                    sbk = (0, 1, 6, 7)[i % 4]
                    pS = pb[sbk]
                    pSk = f"pb{sbk}"
                    pairs = [(khh_[:, c * 2048 + i * 128:
                                   c * 2048 + i * 128 + 128],
                              khl_[:, c * 2048 + i * 128:
                                   c * 2048 + i * 128 + 128],
                              qhh_[:, c * 2048 + J * 512:
                                   c * 2048 + J * 512 + 512],
                              qhl_[:, c * 2048 + J * 512:
                                   c * 2048 + J * 512 + 512])
                             for c in range(2)]
                    A("tensor", mm3(pS, pairs), reads=qk_keys,
                      writes=[pSk], tag=f"s{b}.{h}.{J}.{i}")
                    A("scalar", lambda e, i=i, pS=pS: e.activation(
                        expS(i * 512, (i + 1) * 512), pS[:, 0:512], AF.Exp),
                      reads=[pSk], writes=[f"eS{i}"], tag=f"exp{b}.{h}.{J}.{i}")
                    if i >= 4 * J:
                        dc = (i - 4 * J) * 128
                        A("vector", lambda e, i=i, dc=dc: e.tensor_tensor(
                            expS(i * 512 + dc, i * 512 + dc + 128),
                            expS(i * 512 + dc, i * 512 + dc + 128),
                            dmask[:], op=OP.mult),
                          reads=[f"eS{i}", "dmask"], writes=[f"eS{i}"],
                          tag="diag")
                while pending_trmix:
                    pending_trmix.pop(0)()
                for jj in range(4):
                    jg = 4 * J + jj
                    pA_ = pb[2 + jj % 2]
                    pAk = f"pb{2 + jj % 2}"

                    def mmpv(e, jj=jj, jg=jg, J=J, pA_=pA_):
                        for i in range(jg + 1):
                            inst = e.matmul(
                                pA_[:, 0:HD + 1],
                                expS(i * 512 + jj * 128, i * 512 + jj * 128 + 128),
                                vhb[:, i * (HD + 1):(i + 1) * (HD + 1)],
                                start=(i == 0), stop=(i == jg))
                        return inst
                    A("tensor", mmpv,
                      reads=[f"eS{i}" for i in range(jg + 1)] +
                            [f"vh{i}" for i in range(jg + 1)] +
                            [f"vones{i}" for i in range(jg + 1)],
                      writes=[pAk], tag=f"pv{b}.{h}.{jg}")
                    amb = cB[:, 4096 + (jg % 4) * 256:4096 + (jg % 4) * 256 + 256]
                    A("sync", lambda e, h=h, jg=jg, amb=amb: e.dma_start(
                        out=amb, in_=am_s[h, jg]),
                      writes=[f"amL{jg % 4}"], is_dma=True)
                    cj = 16 + 2 * (jj % 4)
                    A("vector", lambda e, pA_=pA_, cj=cj: e.reciprocal(
                        sp[:, cj:cj + 1], pA_[:, HD:HD + 1]),
                      reads=[pAk], writes=[f"rd{jj}"])
                    A("vector", lambda e, cj=cj, h=h: e.tensor_scalar(
                        sp[:, cj + 1:cj + 2], sp[:, cj:cj + 1],
                        gt[:, H + h:H + h + 1], None, op0=OP.mult),
                      reads=[f"rd{jj}", "gt1m"], writes=[f"rdm{jj}"])
                    mxb = cB[:, 10240 + jj * 256:10240 + jj * 256 + 256]
                    A("vector", lambda e, pA_=pA_, mxb=mxb, amb=amb, cj=cj:
                      e.scalar_tensor_tensor(mxb, pA_[:, 0:HD],
                                             sp[:, cj + 1:cj + 2], amb,
                                             op0=OP.mult, op1=OP.add),
                      reads=[pAk, f"rdm{jj}", f"amL{jg % 4}"],
                      writes=[f"mx{jj}"], tag=f"mix{b}.{h}.{jg}")
                def _trmix_batch(J=J, h=h):
                    for jj in range(4):
                        mxb = cB[:, 10240 + jj * 256:10240 + jj * 256 + 256]
                        for c in range(2):
                            A("tensor", lambda e, mxb=mxb, c=c: e.transpose(
                                pb[4 + c][:, 0:128],
                                mxb[:, c * 128:c * 128 + 128], idn[:]),
                              reads=[f"mx{jj}", "idn"], writes=[f"pb{4 + c}"],
                              tag="trmix")
                            A("vector", lambda e, c=c, jj=jj: e.tensor_copy(
                                mxsh[:, c * 512 + jj * 128:
                                     c * 512 + jj * 128 + 128],
                                pb[4 + c][:, 0:128]),
                              reads=[f"pb{4 + c}"], writes=[f"mxTh{c}"])
                            A("vector", lambda e, c=c, jj=jj: e.tensor_tensor(
                                mxsl[:, c * 512 + jj * 128:
                                     c * 512 + jj * 128 + 128],
                                pb[4 + c][:, 0:128],
                                mxsh[:, c * 512 + jj * 128:
                                     c * 512 + jj * 128 + 128],
                                op=OP.subtract),
                              reads=[f"pb{4 + c}", f"mxTh{c}"],
                              writes=[f"mxTl{c}"])
                    for c in range(2):
                        for hl in range(2):
                            src2 = mxsh if hl == 0 else mxsl
                            sk = f"mxT{'h' if hl == 0 else 'l'}{c}"
                            A("gpsimd", lambda e, h=h, c=c, J=J, hl=hl,
                              src2=src2: e.dma_start(
                                  out=mixT_s[hl, 2 * h + c][:,
                                             J * 512:J * 512 + 512],
                                  in_=src2[:, c * 512:c * 512 + 512]),
                              reads=[sk], writes=["mxd"], is_dma=True,
                              tag=f"stmix{b}.{h}.{J}")
                pending_trmix.append(_trmix_batch)
        while pending_trmix:
            pending_trmix.pop(0)()
        qk_alias = [f"{x}{c}_0" for x in ("qh", "ql", "kh", "kl")
                    for c in range(2)]
        for c in range(NCH):
            for hl, dst in enumerate([woh, wol]):
                A("sync", lambda e, c=c, hl=hl, dst=dst: e.dma_start(
                    out=dst[:, c * 1024:(c + 1) * 1024], in_=wo_hl[hl, c]),
                  writes=[f"wo{'h' if hl == 0 else 'l'}{c}"] + qk_alias,
                  is_dma=True, tag=f"ldwo{c}")
        wo_keys = [f"wo{x}{c}" for x in "hl" for c in range(NCH)]
        mxh0 = cA[:, 0:2048].bitcast(BF16)
        mxl0 = cA[:, 4096:6144].bitcast(BF16)
        for c in range(NCH):
            for hl, dst in enumerate([mxh0, mxl0]):
                es_alias = ([f"eS{c}"] if hl == 0 else [f"eS{8 + c}"])
                A("sync", lambda e, c=c, hl=hl, dst=dst:
                  e.dma_start(out=dst[:, c * 512:(c + 1) * 512],
                              in_=mixT_s[hl, c][:, 0:512]),
                  reads=["mxd"],
                  writes=[f"mxs0_{'h' if hl == 0 else 'l'}{c}"] + es_alias,
                  is_dma=True, tag=f"ldmx{b}.0.{c}")
        P.barrier()
        if stop_after == "B":
            break

        for sl in range(4):
            mxh = cA[:, (sl % 2) * 2048:(sl % 2) * 2048 + 2048].bitcast(BF16)
            mxl = cA[:, 4096 + (sl % 2) * 2048:
                     4096 + (sl % 2) * 2048 + 2048].bitcast(BF16)
            mxk = [f"mxs{sl % 2}_{x}{c}" for x in "hl" for c in range(NCH)]
            if sl > 0:
                for c in range(NCH):
                    for hl, dst in enumerate([mxh, mxl]):
                        A("sync", lambda e, c=c, sl=sl, hl=hl, dst=dst:
                          e.dma_start(
                              out=dst[:, c * 512:(c + 1) * 512],
                              in_=mixT_s[hl, c][:, sl * 512:sl * 512 + 512]),
                          writes=[f"mxs{sl % 2}_{'h' if hl == 0 else 'l'}{c}"],
                          is_dma=True, tag=f"ldmx{b}.{sl}.{c}")
            aoT = cB[:, 0:4096]
            x1 = cB[:, 4096:8192]
            xr_slots = [slice(4096, 5120), slice(5120, 6144),
                        slice(0, 1024), slice(1024, 2048)]
            for tt in range(4):
                ti = sl * 4 + tt
                lnb = cC[:, xr_slots[tt]]
                A("sync", lambda e, ti=ti, lnb=lnb: e.dma_start(
                    out=lnb, in_=ln1_s[ti]),
                  writes=[f"xr{tt}"], is_dma=True)
                A("scalar", lambda e, lnb=lnb, ti=ti: e.activation(
                    lnb, lnb, AF.Identity, bias=sA[:, 32 + ti:33 + ti],
                    scale=sA[:, 48 + ti:49 + ti]),
                  reads=[f"xr{tt}", "sd8", "mean8"],
                  writes=[f"xr{tt}"], tag=f"xrec{b}.{ti}")

            def _trao(ot):
                for tt in range(4):
                    ti = sl * 4 + tt
                    xr = cC[:, xr_slots[tt]]
                    pt_ = pb[2 + (ot * 4 + tt) % 4]
                    ptk = f"pb{2 + (ot * 4 + tt) % 4}"
                    A("tensor", lambda e, ot=ot, tt=tt, pt_=pt_: e.transpose(
                        pt_[:, 0:128],
                        aoT[:, ot * 512 + tt * 128:ot * 512 + tt * 128 + 128],
                        idn[:]),
                      reads=[f"aoT{ot}", "idn"], writes=[ptk], tag="trao")
                    A("vector", lambda e, ot=ot, tt=tt, pt_=pt_, xr=xr:
                      e.tensor_tensor(
                          x1[:, tt * 1024 + ot * 128:tt * 1024 + ot * 128 + 128],
                          pt_[:, 0:128], xr[:, ot * 128:ot * 128 + 128],
                          op=OP.add),
                      reads=[ptk, f"xr{tt}"], writes=[f"x1_{tt}"],
                      tag=f"x1{b}.{ti}")
            for ot in range(NCH):
                pw = pb[ot % 2]
                pwk = f"pb{ot % 2}"
                pairs = [(woh[:, c * 1024 + ot * 128:c * 1024 + ot * 128 + 128],
                          wol[:, c * 1024 + ot * 128:c * 1024 + ot * 128 + 128],
                          mxh[:, c * 512:(c + 1) * 512],
                          mxl[:, c * 512:(c + 1) * 512])
                         for c in range(NCH)]
                A("tensor", mm3(pw, pairs), reads=wo_keys + mxk, writes=[pwk],
                  tag=f"wo{b}.{sl}.{ot}")
                A("scalar", lambda e, ot=ot, pw=pw: e.activation(
                    aoT[:, ot * 512:ot * 512 + 512], pw[:, 0:512], AF.Identity,
                    bias=boT[:, ot:ot + 1]),
                  reads=[pwk, "boT"], writes=[f"aoT{ot}"])
                if ot >= 1:
                    _trao(ot - 1)
            _trao(NCH - 1)
            for tt in range(4):
                ti = sl * 4 + tt
                A("vector", lambda e, tt=tt: e.reduce_sum(
                    s2[:, tt:tt + 1], x1[:, tt * 1024:(tt + 1) * 1024],
                    axis=AX.X),
                  reads=[f"x1_{tt}"], writes=[f"s2s{tt}"])
                A("scalar", lambda e, tt=tt: e.activation(
                    trash[:], x1[:, tt * 1024:tt * 1024 + 512], AF.Square,
                    accum_out=s2[:, 4 + tt:5 + tt]),
                  reads=[f"x1_{tt}"], writes=[f"s2q{tt}", "trash"])
                A("scalar", lambda e, tt=tt: e.activation(
                    trash[:], x1[:, tt * 1024 + 512:tt * 1024 + 1024],
                    AF.Square, accum_out=s2[:, 24 + tt:25 + tt]),
                  reads=[f"x1_{tt}"], writes=[f"s2qh{tt}", "trash"])
            A("vector", lambda e: e.tensor_scalar(
                s2[:, 8:12], s2[:, 0:4], 1.0 / D, None, op0=OP.mult),
              reads=[f"s2s{t}" for t in range(4)], writes=["m2"])
            A("vector", lambda e: e.tensor_tensor(
                s2[:, 4:8], s2[:, 4:8], s2[:, 24:28], op=OP.add),
              reads=[f"s2q{t}" for t in range(4)] +
                    [f"s2qh{t}" for t in range(4)],
              writes=["s2qtot"])
            A("vector", lambda e: e.tensor_scalar(
                s2[:, 4:8], s2[:, 4:8], 1.0 / D, None, op0=OP.mult),
              reads=["s2qtot"], writes=["e2"])
            A("vector", lambda e: e.tensor_tensor(
                s2[:, 20:24], s2[:, 8:12], s2[:, 8:12], op=OP.mult),
              reads=["m2"], writes=["ms2"])
            A("vector", lambda e: e.scalar_tensor_tensor(
                s2[:, 20:24], s2[:, 4:8], eps, s2[:, 20:24],
                op0=OP.add, op1=OP.subtract),
              reads=["e2", "ms2"], writes=["ve2"])
            A("scalar", lambda e: e.activation(s2[:, 4:8], s2[:, 20:24],
                                               AF.Sqrt),
              reads=["ve2"], writes=["sd2"])
            A("vector", lambda e: e.reciprocal(s2[:, 12:16], s2[:, 4:8]),
              reads=["sd2"], writes=["r2"])
            ln2Th = cB[:, 8192:10240].bitcast(BF16)
            ln2Tl = cB[:, 10240:12288].bitcast(BF16)
            for tt in range(4):
                lnb = cC[:, 2048 + (tt % 2) * 1024:2048 + (tt % 2) * 1024 + 1024]
                A("vector", lambda e, tt=tt, lnb=lnb: e.tensor_scalar(
                    lnb, x1[:, tt * 1024:(tt + 1) * 1024],
                    s2[:, 8 + tt:9 + tt], s2[:, 12 + tt:13 + tt],
                    op0=OP.subtract, op1=OP.mult),
                  reads=[f"x1_{tt}", "m2", "r2"], writes=[f"ln2_{tt % 2}"],
                  tag=f"ln2{b}.{sl}.{tt}")
                for c in range(NCH):
                    pt_ = pb[6 + c % 2]
                    ptk = f"pb{6 + c % 2}"
                    A("tensor", lambda e, lnb=lnb, c=c, pt_=pt_: e.transpose(
                        pt_[:, 0:128], lnb[:, c * 128:c * 128 + 128], idn[:]),
                      reads=[f"ln2_{tt % 2}", "idn"], writes=[ptk],
                      tag="trln2")
                    A("scalar", lambda e, c=c, tt=tt, pt_=pt_: e.copy(
                        ln2Th[:, c * 512 + tt * 128:c * 512 + tt * 128 + 128],
                        pt_[:, 0:128]),
                      reads=[ptk], writes=[f"ln2Th{c}"])
                    A("vector", lambda e, c=c, tt=tt, pt_=pt_: e.tensor_tensor(
                        ln2Tl[:, c * 512 + tt * 128:c * 512 + tt * 128 + 128],
                        pt_[:, 0:128],
                        ln2Th[:, c * 512 + tt * 128:c * 512 + tt * 128 + 128],
                        op=OP.subtract),
                      reads=[ptk, f"ln2Th{c}"], writes=[f"ln2Tl{c}"])
            ln2T_keys = [f"ln2T{x}{c}" for x in "hl" for c in range(NCH)]
            ffnT = aoT
            for ot in range(NCH):
                pw = pb[ot % 2]
                pwk = f"pb{ot % 2}"
                pairs = [(mTh[:, c * D + ot * 128:c * D + ot * 128 + 128],
                          mTl[:, c * D + ot * 128:c * D + ot * 128 + 128],
                          ln2Th[:, c * 512:(c + 1) * 512],
                          ln2Tl[:, c * 512:(c + 1) * 512])
                         for c in range(NCH)]
                A("tensor", mm3(pw, pairs), reads=ln2T_keys, writes=[pwk],
                  tag=f"ffn{b}.{sl}.{ot}")
                A("scalar", lambda e, ot=ot, pw=pw: e.activation(
                    ffnT[:, ot * 512:ot * 512 + 512], pw[:, 0:512], AF.Identity,
                    bias=beT[:, ot:ot + 1]),
                  reads=[pwk, "beT"], writes=[f"ffnT{ot}"])
            for tt in range(4):
                ti = sl * 4 + tt
                ob = cC[:, 6144 + (tt % 2) * 1024:6144 + (tt % 2) * 1024 + 1024]
                for ot in range(NCH):
                    pt_ = pb[2 + ot % 4]
                    ptk = f"pb{2 + ot % 4}"
                    A("tensor", lambda e, ot=ot, tt=tt, pt_=pt_: e.transpose(
                        pt_[:, 0:128],
                        ffnT[:, ot * 512 + tt * 128:ot * 512 + tt * 128 + 128],
                        idn[:]),
                      reads=[f"ffnT{ot}", "idn"], writes=[ptk], tag="trffn")
                    A("vector", lambda e, ot=ot, tt=tt, pt_=pt_, ob=ob:
                      e.tensor_tensor(
                          ob[:, ot * 128:ot * 128 + 128], pt_[:, 0:128],
                          x1[:, tt * 1024 + ot * 128:tt * 1024 + ot * 128 + 128],
                          op=OP.add),
                      reads=[ptk, f"x1_{tt}"], writes=[f"ob{tt % 2}"],
                      tag=f"out{b}.{ti}")
                A("gpsimd", lambda e, ob=ob, b=b, ti=ti: e.dma_start(
                    out=y_r[b, ti], in_=ob),
                  reads=[f"ob{tt % 2}"], writes=[f"yd{b}_{ti}"], is_dma=True,
                  tag=f"sty{b}.{ti}")
        if b == 0:
            for i in range(2):
                A("sync", lambda e, i=i: e.dma_start(
                    out=cB[:, 6144 + i * 1024:6144 + (i + 1) * 1024],
                    in_=x_r[1, i]),
                  writes=[f"xs{i}", f"x1_{2 + i}"], is_dma=True,
                  tag=f"hoistx1.{i}")
        wo_alias = [f"wo{x}{c}" for x in "hl" for c in range(NCH)]
        for c in range(NCH):
            A("sync", lambda e, c=c: e.dma_start(
                out=wth[:, c * 1024:(c + 1) * 1024], in_=wq_hl[0, c]),
              writes=[f"wth{c}"] + (wo_alias if c == 0 else []), is_dma=True,
              tag=f"hoistwq{c}")
            A("sync", lambda e, c=c: e.dma_start(
                out=wtl[:, c * 1024:(c + 1) * 1024], in_=wq_hl[1, c]),
              writes=[f"wtl{c}"], is_dma=True)
        P.barrier()

    if stop_after is None:
        out_keys = [f"yd{b}_{ti}" for b in range(B) for ti in range(NT)]
        A("gpsimd", lambda e: None, reads=out_keys, tag="final_wait")

    with nc.Block() as block:
        P.emit(block, sems)
    es.close()
    return nc


_NC_CACHE = None


def kernel(L0_attn_hidden_states, L0_attn_Wq, L0_attn_Wk, L0_attn_Wv,
           L0_attn_Wo, L0_attn_bo, L0_attn_betas, L0_attn_mem, L0_attn_z,
           L0_ffn_W1, L0_ffn_b1, L0_ffn_W2, L0_ffn_b2):
    global _NC_CACHE
    f = np.float32
    x = np.ascontiguousarray(L0_attn_hidden_states, dtype=f)
    wqt = np.ascontiguousarray(np.asarray(L0_attn_Wq, dtype=f).T)
    wkt = np.ascontiguousarray(np.asarray(L0_attn_Wk, dtype=f).T)
    wvt = np.ascontiguousarray(np.asarray(L0_attn_Wv, dtype=f).T)
    wot = np.ascontiguousarray(np.asarray(L0_attn_Wo, dtype=f).T)
    bo = np.ascontiguousarray(L0_attn_bo, dtype=f)
    betas = np.ascontiguousarray(np.asarray(L0_attn_betas, dtype=f)
                                 .reshape(1, 4))
    memz = np.ascontiguousarray(np.concatenate(
        [np.asarray(L0_attn_mem, dtype=f),
         np.asarray(L0_attn_z, dtype=f)], axis=2))
    w1 = np.asarray(L0_ffn_W1, dtype=f)
    b1 = np.asarray(L0_ffn_b1, dtype=f)
    w2 = np.asarray(L0_ffn_W2, dtype=f)
    b2 = np.asarray(L0_ffn_b2, dtype=f)
    idn = np.eye(128, dtype=f)
    dmask = np.triu(np.ones((128, 128), dtype=f))

    if _NC_CACHE is None:
        _NC_CACHE = _build_nc()
    nc = _NC_CACHE

    in_maps = []
    for c in range(8):
        in_maps.append({
            "x": np.ascontiguousarray(x[2 * c:2 * c + 2]),
            "wqt": wqt, "wkt": wkt, "wvt": wvt, "wot": wot,
            "bo": bo, "betas": betas, "memz": memz,
            "w1": np.ascontiguousarray(w1[c]),
            "w2t": np.ascontiguousarray(w2[c].T),
            "b1": np.ascontiguousarray(b1[c]),
            "b2": np.ascontiguousarray(b2[c]),
            "idn": idn, "dmask": dmask,
        })
    res = run_bass_kernel_spmd(nc, in_maps, core_ids=list(range(8)))
    out = np.concatenate([np.asarray(r["y"]) for r in res.results], axis=0)
    return out.astype(np.float32)


# version 18
# speedup vs baseline: 1.0278x; 1.0278x over previous
import numpy as np
import concourse.bass as bass
import concourse.mybir as mybir
from concourse.bass_utils import run_bass_kernel_spmd

F32 = mybir.dt.float32
BF16 = mybir.dt.bfloat16
AF = mybir.ActivationFunctionType
OP = mybir.AluOpType
AX = mybir.AxisListType

ENGINES = ("sync", "scalar", "vector", "tensor", "gpsimd")


DMA_SLOTS = 8


class _Op:
    __slots__ = ("idx", "engine", "fn", "reads", "writes", "is_dma",
                 "token_needed", "deps", "tag", "sem_name")

    def __init__(self, idx, engine, fn, reads, writes, is_dma, tag=""):
        self.idx = idx
        self.engine = engine
        self.fn = fn
        self.reads = tuple(reads)
        self.writes = tuple(writes)
        self.is_dma = is_dma
        self.token_needed = False
        self.deps = []
        self.tag = tag
        self.sem_name = engine


class _Prog:

    def __init__(self):
        self.ops = []
        self.last_writer = {}
        self.readers = {}
        self.last_on = {}
        self.dma_n = {}
        self.slot_last = {}

    def add(self, engine, fn, reads=(), writes=(), is_dma=False, tag=""):
        op = _Op(len(self.ops), engine, fn, reads, writes, is_dma, tag)
        deps = {}
        for k in op.reads:
            w = self.last_writer.get(k)
            if w is not None:
                deps[w.idx] = w
            if k.startswith("pb") and engine in ("scalar", "vector"):
                for r in self.readers.get(k, ()):
                    if r.engine in ("scalar", "vector") and \
                            r.engine != engine:
                        deps[r.idx] = r
            self.readers.setdefault(k, []).append(op)
        for k in op.writes:
            w = self.last_writer.get(k)
            if w is not None:
                deps[w.idx] = w
            for r in self.readers.get(k, ()):
                if r.engine != op.engine:
                    deps[r.idx] = r
        for k in op.writes:
            self.last_writer[k] = op
            self.readers[k] = []
        if is_dma:
            n = self.dma_n.get(engine, 0)
            self.dma_n[engine] = n + 1
            slot = n % DMA_SLOTS
            op.sem_name = f"{engine}_d{slot}"
            prev = self.slot_last.get(op.sem_name)
            if prev is not None:
                deps[prev.idx] = prev
            self.slot_last[op.sem_name] = op
        deps.pop(op.idx, None)
        op.deps = list(deps.values())
        self.ops.append(op)
        self.last_on[engine] = op
        return op

    def barrier(self):
        lasts = list(self.last_on.values()) + list(self.slot_last.values())
        for e in ENGINES:
            deps = [o for o in lasts if o is not self.last_on.get(e)]
            if not deps:
                continue
            op = _Op(len(self.ops), e, lambda eng: None, (), (), False,
                     tag="barrier")
            op.deps = deps
            self.ops.append(op)
            self.last_on[e] = op
        self.last_writer.clear()
        self.readers.clear()

    def sem_names(self):
        names = set(ENGINES)
        for q in ("sync", "gpsimd", "scalar", "vector"):
            for s in range(DMA_SLOTS):
                names.add(f"{q}_d{s}")
        return sorted(names)

    def emit(self, block, sems):
        for op in self.ops:
            if op.is_dma:
                op.token_needed = True
            for d in op.deps:
                d.token_needed = True
        counts = {}
        tokval = {}
        for op in self.ops:
            if op.token_needed:
                c = counts.get(op.sem_name, 0) + (16 if op.is_dma else 1)
                counts[op.sem_name] = c
                tokval[op.idx] = c
        per_engine = {e: [] for e in ENGINES}
        for op in self.ops:
            per_engine[op.engine].append(op)

        def mk(ename):
            ops_list = per_engine[ename]

            def closure(eng):
                waited = {}
                for op in ops_list:
                    for d in op.deps:
                        v = tokval[d.idx]
                        if waited.get(d.sem_name, -1) < v:
                            eng.wait_ge(sems[d.sem_name], v)
                            waited[d.sem_name] = v
                    inst = op.fn(eng)
                    if op.token_needed:
                        inst.then_inc(sems[op.sem_name],
                                      16 if op.is_dma else 1)
            return closure

        for e in ENGINES:
            if per_engine[e]:
                getattr(block, e)(mk(e))


def _build_nc(stop_after=None):
    B, S, D, H, HD = 2, 2048, 1024, 4, 256
    NT = S // 128
    NCH = D // 128
    INTER = 4096

    nc = bass.Bass("TRN2", target_bir_lowering=False)

    def dram(n, sh, kind=None, dt=F32):
        t = (nc.dram_tensor(n, sh, dt) if kind is None
             else nc.dram_tensor(n, sh, dt, kind=kind))
        return t.ap()

    x_in = dram("x", [B, S, D], "ExternalInput")
    wqt = dram("wqt", [D, D], "ExternalInput")
    wkt = dram("wkt", [D, D], "ExternalInput")
    wvt = dram("wvt", [D, D], "ExternalInput")
    wot = dram("wot", [D, D], "ExternalInput")
    bo_in = dram("bo", [D], "ExternalInput")
    beta_in = dram("betas", [1, H], "ExternalInput")
    memz_in = dram("memz", [H, HD, HD + 1], "ExternalInput")
    w1_in = dram("w1", [INTER, D], "ExternalInput")
    w2t_in = dram("w2t", [INTER, D], "ExternalInput")
    b1_in = dram("b1", [INTER], "ExternalInput")
    b2_in = dram("b2", [D], "ExternalInput")
    idn_in = dram("idn", [128, 128], "ExternalInput")
    dmask_in = dram("dmask", [128, 128], "ExternalInput")
    y_out = dram("y", [B, S, D], "ExternalOutput")

    qT_s = dram("qT_s", [2, NCH, 128, S], dt=BF16)
    kT_s = dram("kT_s", [2, NCH, 128, S], dt=BF16)
    v_s = dram("v_s", [NT, 128, D])
    am_s = dram("am_s", [H, NT, 128, HD])
    ln1_s = dram("ln1_s", [NT, 128, D])
    mixT_s = dram("mixT_s", [2, NCH, 128, S], dt=BF16)
    be_s = dram("be_s", [1, D])
    wq_hl = dram("wq_hl", [2, NCH, 128, D], dt=BF16)
    wk_hl = dram("wk_hl", [2, NCH, 128, D], dt=BF16)
    wv_hl = dram("wv_hl", [2, NCH, 128, D], dt=BF16)
    wo_hl = dram("wo_hl", [2, NCH, 128, D], dt=BF16)

    x_r = x_in.rearrange("b (t p) d -> b t p d", p=128)
    y_r = y_out.rearrange("b (t p) d -> b t p d", p=128)
    wqt_r = wqt.rearrange("(c p) o -> c p o", p=128)
    wkt_r = wkt.rearrange("(c p) o -> c p o", p=128)
    wvt_r = wvt.rearrange("(c p) o -> c p o", p=128)
    wot_r = wot.rearrange("(c p) o -> c p o", p=128)
    bo_r = bo_in.rearrange("(c p) -> p c", p=128)
    b2_r = b2_in.rearrange("(c p) -> p c", p=128)
    b1_r = b1_in.rearrange("(c p) -> p c", p=128)
    be_r = be_s.rearrange("a (c p) -> (a p) c", p=128)
    memz_r = memz_in.rearrange("h (c p) e -> h c p e", p=128)
    w1_r = w1_in.rearrange("(c p) d -> c p d", p=128)
    w2t_r = w2t_in.rearrange("(c p) d -> c p d", p=128)

    from contextlib import ExitStack
    es = ExitStack()
    sb = lambda n, w: es.enter_context(nc.sbuf_tensor(n, [128, w], F32))
    mT = sb("mT", NCH * D)
    vhb = sb("vhb", NT * (HD + 1))

    class _Split:
        def __init__(self, parts, w=4096):
            self.parts = parts
            self.w = w

        def __getitem__(self, key):
            ps, cs = key
            lo, hi = cs.start, cs.stop
            i = lo // self.w
            assert hi <= (i + 1) * self.w, (lo, hi)
            return self.parts[i][ps, lo - i * self.w:hi - i * self.w]

    cA = _Split([sb(f"cA{i}", 4096) for i in range(4)])
    cB = _Split([sb(f"cB{i}", 4096) for i in range(3)])
    cC = _Split([sb(f"cC{i}", 4096) for i in range(2)])
    idn = sb("idn_t", 128)
    dmask = sb("dmask_t", 128)
    memz = sb("memz_t", 2 * H * (HD + 1))
    boT = sb("boT", NCH)
    beT = sb("beT", NCH)
    b2T = sb("b2T", NCH)
    b1s = sb("b1s", INTER // 128)
    gt = sb("gt", 2 * H)
    sA = sb("sA", 64)
    s2 = sb("s2", 32)
    sp = sb("sp", 64)
    trash = sb("trash", 512)
    pb = [es.enter_context(nc.psum_tensor(f"pb{i}", [128, 512], F32))
          for i in range(8)]

    mTh = mT[:, 0:4096].bitcast(BF16)
    mTl = mT[:, 4096:8192].bitcast(BF16)
    lnTh = cA[:, 0:4096].bitcast(BF16)
    lnTl = cA[:, 4096:8192].bitcast(BF16)
    wth = cA[:, 8192:12288].bitcast(BF16)
    wtl = cA[:, 12288:16384].bitcast(BF16)
    def expS(lo, hi):
        return cA[:, lo:hi]
    qhh = cA[:, 8192:10240].bitcast(BF16)
    qhl = cA[:, 10240:12288].bitcast(BF16)
    khh = cA[:, 12288:14336].bitcast(BF16)
    khl = cA[:, 14336:16384].bitcast(BF16)
    woh = cA[:, 8192:12288].bitcast(BF16)
    wol = cA[:, 12288:16384].bitcast(BF16)
    ones1 = cB[0:1, 1028:1156]
    brow = cB[0:1, 0:1024]
    betar = cB[0:1, 1024:1028]

    P = _Prog()
    sems = {}
    for name in P.sem_names():
        sems[name] = es.enter_context(nc.semaphore(f"s_{name}"))
    A = P.add

    A("sync", lambda e: e.dma_start(out=idn[:], in_=idn_in[:]),
      writes=["idn"], is_dma=True, tag="ld_idn")
    A("sync", lambda e: e.dma_start(out=dmask[:], in_=dmask_in[:]),
      writes=["dmask"], is_dma=True, tag="ld_dmask")
    for h in range(H):
        for c in range(2):
            A("sync", lambda e, h=h, c=c: e.dma_start(
                out=memz[:, (2 * h + c) * (HD + 1):(2 * h + c + 1) * (HD + 1)],
                in_=memz_r[h, c]),
              writes=["memz"], is_dma=True, tag="ld_memz")
    A("sync", lambda e: e.dma_start(allow_slow_non_contiguous=True,
                                    out=boT[:], in_=bo_r[:]),
      writes=["boT"], is_dma=True)
    A("sync", lambda e: e.dma_start(allow_slow_non_contiguous=True,
                                    out=b2T[:], in_=b2_r[:]),
      writes=["b2T"], is_dma=True)
    A("sync", lambda e: e.dma_start(allow_slow_non_contiguous=True,
                                    out=b1s[:], in_=b1_r[:]),
      writes=["b1s"], is_dma=True)
    A("sync", lambda e: e.dma_start(out=betar[:], in_=beta_in[:]),
      writes=["betar"], is_dma=True)
    for i in range(NT):
        A("gpsimd", lambda e, i=i: e.memset(
            vhb[:, i * (HD + 1) + HD:(i + 1) * (HD + 1)], 1.0),
          writes=[f"vones{i}"])
    A("gpsimd", lambda e: e.memset(ones1[:], 1.0), writes=["ones1"])
    A("gpsimd", lambda e: e.memset(sp[:, 63:64], -20.0), writes=["eshift"])

    A("scalar", lambda e: e.activation(betar[:], betar[:], AF.Sigmoid),
      reads=["betar"], writes=["betar"])
    A("tensor", lambda e: e.matmul(pb[0][:, 0:H], ones1[:], betar[:],
                                   start=True, stop=True),
      reads=["betar", "ones1"], writes=["pb0"], tag="gbc")
    A("vector", lambda e: e.tensor_copy(gt[:, 0:H], pb[0][:, 0:H]),
      reads=["pb0"], writes=["gt"])
    A("vector", lambda e: e.tensor_scalar(gt[:, H:2 * H], gt[:, 0:H],
                                          -1.0, 1.0, op0=OP.mult, op1=OP.add),
      reads=["gt"], writes=["gt1m"])

    NI = INTER // 128
    wsin = [cC[:, 5120:6144], cC[:, 5120:6144]]
    wsh = cC[:, 7168:7680].bitcast(BF16)
    wsl = cC[:, 7680:8192].bitcast(BF16)

    def _wsplit_steps():
        for wi, (wt_r, whl) in enumerate([(wqt_r, wq_hl), (wkt_r, wk_hl),
                                          (wvt_r, wv_hl), (wot_r, wo_hl)]):
            for c in range(NCH):
                yield (wi, c, wt_r, whl)

    def _emit_wsplit(step):
        wi, c, wt_r, whl = step
        k = 0
        buf = wsin[k]
        A("sync", lambda e, buf=buf, wt_r=wt_r, c=c: e.dma_start(
            out=buf, in_=wt_r[c]),
          writes=[f"wsin{k}"], is_dma=True, tag=f"ldws{wi}.{c}")
        A("scalar", lambda e, buf=buf: e.copy(wsh[:], buf),
          reads=[f"wsin{k}"], writes=["wsh"], tag="wsplit_h")
        A("vector", lambda e, buf=buf: e.tensor_tensor(
            wsl[:], buf, wsh[:], op=OP.subtract),
          reads=[f"wsin{k}", "wsh"], writes=["wsl"], tag="wsplit_l")
        A("gpsimd", lambda e, whl=whl, c=c: e.dma_start(
            out=whl[0, c], in_=wsh[:]),
          reads=["wsh"], writes=["whld"], is_dma=True)
        A("gpsimd", lambda e, whl=whl, c=c: e.dma_start(
            out=whl[1, c], in_=wsl[:]),
          reads=["wsl"], writes=["whld"], is_dma=True)

    _wsteps = iter(_wsplit_steps())
    for c in range(NI):
        buf = cC[:, (c % 2) * 1024:(c % 2) * 1024 + 1024]
        A("sync", lambda e, c=c, buf=buf: e.dma_start(out=buf, in_=w2t_r[c]),
          writes=[f"w2tc{c % 2}"], is_dma=True, tag=f"ld_w2t{c}")
        for half in range(2):
            A("tensor", lambda e, c=c, buf=buf, half=half: e.matmul(
                pb[6 + half][0:1, 0:512], b1s[:, c:c + 1],
                buf[:, half * 512:half * 512 + 512],
                start=(c == 0), stop=(c == NI - 1)),
              reads=[f"w2tc{c % 2}", "b1s"], writes=[f"pb{6 + half}"],
              tag=f"be{c}.{half}")
    for half in range(2):
        A("vector", lambda e, half=half: e.tensor_copy(
            brow[:, half * 512:half * 512 + 512], pb[6 + half][0:1, 0:512]),
          reads=[f"pb{6 + half}"], writes=["brow"])
    A("gpsimd", lambda e: e.dma_start(out=be_s[:], in_=brow[:]),
      reads=["brow"], writes=["be_d"], is_dma=True)
    A("sync", lambda e: e.dma_start(allow_slow_non_contiguous=True,
                                    out=beT[:], in_=be_r[:]),
      reads=["be_d"], writes=["beT"], is_dma=True)
    A("vector", lambda e: e.tensor_tensor(beT[:], beT[:], b2T[:], op=OP.add),
      reads=["beT", "b2T"], writes=["beT"])

    w1hb = [cC[:, 0 + i * 512:512 + i * 512].bitcast(BF16) for i in range(2)]
    w1lb = [cC[:, 1024 + i * 512:1536 + i * 512].bitcast(BF16)
            for i in range(2)]
    w2hb = [cC[:, 6144 + i * 256:6400 + i * 256].bitcast(BF16)
            for i in range(2)]
    w2lb = [cC[:, 6656 + i * 256:6912 + i * 256].bitcast(BF16)
            for i in range(2)]
    for osl in range(2):
        for c in range(NI):
            s = c % 2
            w1b = cC[:, 2048 + s * 1024:2048 + s * 1024 + 1024]
            w2b = cC[:, 4096 + s * 512:4096 + s * 512 + 512]
            A("sync", lambda e, c=c, w1b=w1b: e.dma_start(out=w1b,
                                                          in_=w1_r[c]),
              writes=[f"w1c{s}", "w2tc0" if s == 0 else "w2tc1"],
              is_dma=True, tag=f"ld_w1_{osl}_{c}")
            A("sync", lambda e, c=c, w2b=w2b, osl=osl: e.dma_start(
                out=w2b, in_=w2t_r[c][:, osl * 512:osl * 512 + 512]),
              writes=[f"w2c{s}"], is_dma=True)
            A("scalar", lambda e, s=s, w1b=w1b: e.copy(w1hb[s][:], w1b),
              reads=[f"w1c{s}"], writes=[f"w1h{s}", "w2tc0" if s == 0
                                         else "w2tc1"])
            A("vector", lambda e, s=s, w1b=w1b: e.tensor_tensor(
                w1lb[s][:], w1b, w1hb[s][:], op=OP.subtract),
              reads=[f"w1c{s}", f"w1h{s}"], writes=[f"w1l{s}"])
            A("scalar", lambda e, s=s, w2b=w2b: e.copy(w2hb[s][:], w2b),
              reads=[f"w2c{s}"], writes=[f"w2h{s}"])
            A("vector", lambda e, s=s, w2b=w2b: e.tensor_tensor(
                w2lb[s][:], w2b, w2hb[s][:], op=OP.subtract),
              reads=[f"w2c{s}", f"w2h{s}"], writes=[f"w2l{s}"])

            def mmg(e, c=c, s=s):
                n = 0
                for p in range(3):
                    a_ = w1hb[s] if p < 2 else w1lb[s]
                    b_ = w2hb[s] if p != 1 else w2lb[s]
                    for it in range(8):
                        inst = e.matmul(pb[it][:, 0:512],
                                        a_[:, it * 128:it * 128 + 128], b_[:],
                                        start=(c == 0 and p == 0),
                                        stop=(c == NI - 1 and p == 2))
                        n += 1
                return inst
            A("tensor", mmg,
              reads=[f"w1h{s}", f"w1l{s}", f"w2h{s}", f"w2l{s}"],
              writes=[f"pb{it}" for it in range(8)], tag=f"mT{osl}.{c}")
            ws = next(_wsteps, None)
            if ws is not None:
                _emit_wsplit(ws)
        for it in range(8):
            A("scalar", lambda e, it=it, osl=osl: e.copy(
                mTh[:, it * D + osl * 512:it * D + osl * 512 + 512],
                pb[it][:, 0:512]),
              reads=[f"pb{it}"], writes=[f"mTh{it}"], tag=f"evmTh{osl}.{it}")
            A("vector", lambda e, it=it, osl=osl: e.tensor_tensor(
                mTl[:, it * D + osl * 512:it * D + osl * 512 + 512],
                pb[it][:, 0:512],
                mTh[:, it * D + osl * 512:it * D + osl * 512 + 512],
                op=OP.subtract),
              reads=[f"pb{it}", f"mTh{it}"], writes=[f"mTl{it}"],
              tag=f"evmTl{osl}.{it}")
    for ws in _wsteps:
        _emit_wsplit(ws)
    for i in range(2):
        A("sync", lambda e, i=i: e.dma_start(
            out=cB[:, 6144 + i * 1024:6144 + (i + 1) * 1024], in_=x_r[0, i]),
          writes=[f"xs{i}"], is_dma=True, tag=f"hoistx0.{i}")
    for c in range(NCH):
        A("sync", lambda e, c=c: e.dma_start(
            out=wth[:, c * 1024:(c + 1) * 1024], in_=wq_hl[0, c]),
          writes=[f"wth{c}"], is_dma=True, tag=f"hoistwq0.{c}")
        A("sync", lambda e, c=c: e.dma_start(
            out=wtl[:, c * 1024:(c + 1) * 1024], in_=wq_hl[1, c]),
          writes=[f"wtl{c}"], is_dma=True)
    P.barrier()

    def mm3(pq, pairs, n_free=512):
        def fn(e):
            n = len(pairs) * 3
            i = 0
            for p in range(3):
                for (lh, ll, rh, rl) in pairs:
                    a = lh if p < 2 else ll
                    bb = rh if p != 1 else rl
                    inst = e.matmul(pq[:, 0:n_free], a, bb,
                                    start=(i == 0), stop=(i == n - 1))
                    i += 1
            return inst
        return fn

    eps = 1e-5
    if stop_after == "p0":
        with nc.Block() as block:
            P.emit(block, sems)
        es.close()
        return nc
    for b in range(B):
        for hf in range(2):
            xst = [cB[:, 6144 + i * 1024:6144 + (i + 1) * 1024]
                   for i in range(2)]
            for tt in range(8):
                ti = hf * 8 + tt
                xb = xst[tt % 2]
                if not (hf == 0 and tt < 2):
                    A("sync", lambda e, b=b, ti=ti, xb=xb: e.dma_start(
                        out=xb, in_=x_r[b, ti]),
                      writes=[f"xs{tt % 2}"], is_dma=True, tag=f"ldx{b}.{ti}")
                A("vector", lambda e, xb=xb, ti=ti: e.reduce_sum(
                    sA[:, ti:ti + 1], xb, axis=AX.X),
                  reads=[f"xs{tt % 2}"], writes=[f"sum{ti}"])
                A("scalar", lambda e, xb=xb, ti=ti: e.activation(
                    trash[:], xb[:, 0:512], AF.Square,
                    accum_out=sA[:, 16 + ti:17 + ti]),
                  reads=[f"xs{tt % 2}"], writes=[f"ssq{ti}", "trash"])
                A("scalar", lambda e, xb=xb, tt=tt: e.activation(
                    trash[:], xb[:, 512:1024], AF.Square,
                    accum_out=s2[:, 16 + tt:17 + tt]),
                  reads=[f"xs{tt % 2}"], writes=[f"ssqh{tt}", "trash"])
            sl8 = slice(hf * 8, hf * 8 + 8)
            A("vector", lambda e, sl8=sl8, hf=hf: e.tensor_scalar(
                sA[:, 32 + hf * 8:40 + hf * 8], sA[:, sl8], 1.0 / D, None,
                op0=OP.mult),
              reads=[f"sum{hf * 8 + i}" for i in range(8)],
              writes=["mean8"], tag="mean")
            A("vector", lambda e, hf=hf: e.tensor_tensor(
                sA[:, 16 + hf * 8:24 + hf * 8], sA[:, 16 + hf * 8:24 + hf * 8],
                s2[:, 16:24], op=OP.add),
              reads=[f"ssq{hf * 8 + i}" for i in range(8)] +
                    [f"ssqh{i}" for i in range(8)],
              writes=["ssqtot"])
            A("vector", lambda e, hf=hf: e.tensor_scalar(
                sA[:, 16 + hf * 8:24 + hf * 8], sA[:, 16 + hf * 8:24 + hf * 8],
                1.0 / D, None, op0=OP.mult),
              reads=["ssqtot"], writes=["ex28"])
            A("vector", lambda e, hf=hf: e.tensor_tensor(
                s2[:, 24:32], sA[:, 32 + hf * 8:40 + hf * 8],
                sA[:, 32 + hf * 8:40 + hf * 8], op=OP.mult),
              reads=["mean8"], writes=["msq8"])
            A("vector", lambda e, hf=hf: e.scalar_tensor_tensor(
                s2[:, 24:32], sA[:, 16 + hf * 8:24 + hf * 8], eps,
                s2[:, 24:32], op0=OP.add, op1=OP.subtract),
              reads=["ex28", "msq8"], writes=["veps8"])
            A("scalar", lambda e, hf=hf: e.activation(
                sA[:, 48 + hf * 8:56 + hf * 8], s2[:, 24:32], AF.Sqrt),
              reads=["veps8"], writes=["sd8"], tag="sd")
            A("vector", lambda e, hf=hf: e.reciprocal(
                sA[:, 16 + hf * 8:24 + hf * 8], sA[:, 48 + hf * 8:56 + hf * 8]),
              reads=["sd8"], writes=["rstd8"])
            for tt in range(8):
                ti = hf * 8 + tt
                xb = xst[tt % 2]
                lnb = cB[:, 8192 + (tt % 2) * 1024:8192 + (tt % 2 + 1) * 1024]
                A("sync", lambda e, b=b, ti=ti, xb=xb: e.dma_start(
                    out=xb, in_=x_r[b, ti]),
                  writes=[f"xs{tt % 2}"], is_dma=True)
                A("vector", lambda e, xb=xb, lnb=lnb, ti=ti: e.tensor_scalar(
                    lnb, xb, sA[:, 32 + ti:33 + ti], sA[:, 16 + ti:17 + ti],
                    op0=OP.subtract, op1=OP.mult),
                  reads=[f"xs{tt % 2}", "mean8", "rstd8"],
                  writes=[f"ln1t{tt % 2}"], tag=f"ln1_{b}.{ti}")
                A("gpsimd", lambda e, lnb=lnb, ti=ti: e.dma_start(
                    out=ln1_s[ti], in_=lnb),
                  reads=[f"ln1t{tt % 2}"], writes=[f"ln1d{ti}"], is_dma=True)
                for c in range(NCH):
                    pbk = c % 4
                    dst = tt * 128
                    A("tensor", lambda e, lnb=lnb, c=c, pbk=pbk: e.transpose(
                        pb[pbk][:, 0:128], lnb[:, c * 128:c * 128 + 128],
                        idn[:]),
                      reads=[f"ln1t{tt % 2}", "idn"], writes=[f"pb{pbk}"],
                      tag=f"trln{ti}.{c}")
                    hg = tt // 4
                    A("scalar", lambda e, c=c, dst=dst, pbk=pbk: e.copy(
                        lnTh[:, c * 1024 + dst:c * 1024 + dst + 128],
                        pb[pbk][:, 0:128]),
                      reads=[f"pb{pbk}"], writes=[f"lnTh{c}t{hg}"])
                    A("vector", lambda e, c=c, dst=dst, pbk=pbk:
                      e.tensor_tensor(
                          lnTl[:, c * 1024 + dst:c * 1024 + dst + 128],
                          pb[pbk][:, 0:128],
                          lnTh[:, c * 1024 + dst:c * 1024 + dst + 128],
                          op=OP.subtract),
                      reads=[f"pb{pbk}", f"lnTh{c}t{hg}"],
                      writes=[f"lnTl{c}t{hg}"])
            def lnT_keys_for(ts2):
                return [f"lnTh{c}t{ts2}" for c in range(NCH)] + \
                       [f"lnTl{c}t{ts2}" for c in range(NCH)]
            lnT_keys = lnT_keys_for(0) + lnT_keys_for(1)
            qk_h = [cB[:, 3072 + i * 256:3072 + i * 256 + 256].bitcast(BF16)
                    for i in range(2)]
            qk_l = [cB[:, 3584 + i * 256:3584 + i * 256 + 256].bitcast(BF16)
                    for i in range(2)]
            if stop_after == "A1":
                P.barrier()
                break
            if stop_after in ("A1w", "A1m", "A1e", "A1f", "A1g", "A1h"):
                for c in range(NCH):
                    A("sync", lambda e, c=c: e.dma_start(
                        out=wth[:, c * 1024:(c + 1) * 1024], in_=wq_hl[0, c]),
                      writes=[f"wth{c}"], is_dma=True)
                    A("sync", lambda e, c=c: e.dma_start(
                        out=wtl[:, c * 1024:(c + 1) * 1024], in_=wq_hl[1, c]),
                      writes=[f"wtl{c}"], is_dma=True)
                if stop_after in ("A1m", "A1e"):
                    wk = [f"wth{c}" for c in range(NCH)] + \
                         [f"wtl{c}" for c in range(NCH)]
                    lk = [f"lnTh{c}" for c in range(NCH)] + \
                         [f"lnTl{c}" for c in range(NCH)]
                    for h in range(H):
                        for ts_ in range(2):
                            for c01 in range(2):
                                ot = 2 * h + c01
                                pq = pb[4 + c01]
                                pqk = f"pb{4 + c01}"
                                pairs = [(wth[:, c * 1024 + ot * 128:
                                              c * 1024 + ot * 128 + 128],
                                          wtl[:, c * 1024 + ot * 128:
                                              c * 1024 + ot * 128 + 128],
                                          lnTh[:, c * 1024 + ts_ * 512:
                                               c * 1024 + ts_ * 512 + 512],
                                          lnTl[:, c * 1024 + ts_ * 512:
                                               c * 1024 + ts_ * 512 + 512])
                                         for c in range(NCH)]
                                A("tensor", mm3(pq, pairs), reads=wk + lk,
                                  writes=[pqk], tag=f"qm{ot}.{ts_}")
                                ev = cB[:, 2048 + c01 * 512:
                                        2048 + c01 * 512 + 512]
                                A("vector", lambda e, ev=ev, pq=pq:
                                  e.tensor_copy(ev, pq[:, 0:512]),
                                  reads=[pqk], writes=[f"ev{c01}"])
                                if stop_after in ("A1e", "A1f", "A1g", "A1h"):
                                    A("vector", lambda e, pq=pq, c01=c01:
                                      e.tensor_copy(qk_h[c01][:], pq[:, 0:512]),
                                      reads=[pqk], writes=[f"qkh{c01}"])
                                    A("vector", lambda e, pq=pq, c01=c01:
                                      e.tensor_tensor(qk_l[c01][:],
                                                      pq[:, 0:512],
                                                      qk_h[c01][:],
                                                      op=OP.subtract),
                                      reads=[pqk, f"qkl{c01}"],
                                      writes=[f"qkl{c01}"])
                                if stop_after in ("A1f", "A1g", "A1h"):
                                    mb = cB[:, 1024 + c01 * 512:
                                            1024 + c01 * 512 + 512]
                                    mk = f"qm{c01}"
                                    A("vector", lambda e, ev=ev, mb=mb:
                                      e.tensor_scalar(mb, ev, 0.0, None,
                                                      op0=OP.min),
                                      reads=[f"ev{c01}"], writes=[mk])
                                    A("scalar", lambda e, mb=mb: e.activation(
                                        mb, mb, AF.Exp),
                                      reads=[mk], writes=[mk])
                                    sqb = cB[:, c01 * 512:c01 * 512 + 512]
                                    A("vector", lambda e, ev=ev, mb=mb,
                                      sqb=sqb: e.scalar_tensor_tensor(
                                          sqb, ev, 0.0, mb, op0=OP.max,
                                          op1=OP.add),
                                      reads=[f"ev{c01}", mk],
                                      writes=[f"sq{c01}"])
                            if stop_after in ("A1g", "A1h"):
                                for jj in range(4):
                                    pa = pb[6 + jj % 2]
                                    pak = f"pb{6 + jj % 2}"

                                    def mma2(e, h=h, jj=jj, pa=pa):
                                        for c in range(2):
                                            inst = e.matmul(
                                                pa[:, 0:HD + 1],
                                                cB[:, c * 512 + jj * 128:
                                                    c * 512 + jj * 128 + 128],
                                                memz[:, (2 * h + c) * (HD + 1):
                                                     (2 * h + c + 1) * (HD + 1)],
                                                start=(c == 0), stop=(c == 1))
                                        return inst
                                    A("tensor", mma2,
                                      reads=["sq0", "sq1", "memz"],
                                      writes=[pak])
                                    amb = cB[:, 4096 + (jj % 2) * 256:
                                             4096 + (jj % 2) * 256 + 256]
                                    A("vector", lambda e, pa=pa, amb=amb:
                                      e.tensor_scalar(amb, pa[:, 0:HD], 1.0,
                                                      None, op0=OP.mult),
                                      reads=[pak], writes=[f"amv{jj % 2}"])
                                    if stop_after in ("A1h",):
                                        ti2 = hf * 8 + ts_ * 4 + jj
                                        A("gpsimd", lambda e, amb=amb, h=h,
                                          ti2=ti2: e.dma_start(
                                              out=am_s[h, ti2], in_=amb),
                                          reads=[f"amv{jj % 2}"],
                                          writes=["amd"], is_dma=True)
                P.barrier()
                break

            def load_whl(whl, name):
                for c in range(NCH):
                    A("sync", lambda e, c=c, whl=whl: e.dma_start(
                        out=wth[:, c * 1024:(c + 1) * 1024], in_=whl[0, c]),
                      writes=[f"wth{c}"], is_dma=True, tag=f"ld{name}h{c}")
                    A("sync", lambda e, c=c, whl=whl: e.dma_start(
                        out=wtl[:, c * 1024:(c + 1) * 1024], in_=whl[1, c]),
                      writes=[f"wtl{c}"], is_dma=True, tag=f"ld{name}l{c}")
            wt_keys = [f"wth{c}" for c in range(NCH)] + \
                      [f"wtl{c}" for c in range(NCH)]

            if hf == 1:
                load_whl(wq_hl, "q")
            for h in range(H):
                for ts_ in range(2):
                    for c01 in range(2):
                        ot = 2 * h + c01
                        pq = pb[4 + c01]
                        pqk = f"pb{4 + c01}"
                        pairs = [(wth[:, c * 1024 + ot * 128:
                                      c * 1024 + ot * 128 + 128],
                                  wtl[:, c * 1024 + ot * 128:
                                      c * 1024 + ot * 128 + 128],
                                  lnTh[:, c * 1024 + ts_ * 512:
                                       c * 1024 + ts_ * 512 + 512],
                                  lnTl[:, c * 1024 + ts_ * 512:
                                       c * 1024 + ts_ * 512 + 512])
                                 for c in range(NCH)]
                        A("tensor", mm3(pq, pairs),
                          reads=wt_keys + lnT_keys_for(ts_),
                          writes=[pqk], tag=f"q{b}.{hf}.{ot}.{ts_}")
                        ev = cB[:, 2048 + c01 * 512:2048 + c01 * 512 + 512]
                        evk = f"ev{c01}"
                        A("scalar", lambda e, ev=ev, pq=pq: e.copy(
                            ev, pq[:, 0:512]),
                          reads=[pqk], writes=[evk])
                        A("vector", lambda e, pq=pq, c01=c01: e.tensor_copy(
                            qk_h[c01][:], pq[:, 0:512]),
                          reads=[pqk], writes=[f"qkh{c01}"])
                        A("vector", lambda e, pq=pq, c01=c01: e.tensor_tensor(
                            qk_l[c01][:], pq[:, 0:512], qk_h[c01][:],
                            op=OP.subtract),
                          reads=[pqk, f"qkh{c01}"], writes=[f"qkl{c01}"])
                        for hl in range(2):
                            if stop_after in ("A2a", "A2aM1"):
                                break
                            src = qk_h[c01] if hl == 0 else qk_l[c01]
                            sk = f"qk{'h' if hl == 0 else 'l'}{c01}"
                            A("gpsimd", lambda e, src=src, hl=hl, ot=ot,
                              hf=hf, ts_=ts_: e.dma_start(
                                  out=qT_s[hl, ot][:, hf * 1024 + ts_ * 512:
                                                   hf * 1024 + ts_ * 512 + 512],
                                  in_=src[:]),
                              reads=[sk], writes=["qTd"], is_dma=True)
                        mb = cB[:, 1024 + c01 * 512:1024 + c01 * 512 + 512]
                        mk = f"qm{c01}"
                        A("vector", lambda e, ev=ev, mb=mb: e.tensor_scalar(
                            mb, ev, 0.0, None, op0=OP.min),
                          reads=[evk], writes=[mk])
                        A("scalar", lambda e, mb=mb: e.activation(
                            mb, mb, AF.Exp),
                          reads=[mk], writes=[mk], tag="eluexp")
                        sqb = cB[:, c01 * 512:c01 * 512 + 512]
                        A("vector", lambda e, ev=ev, mb=mb, sqb=sqb:
                          e.scalar_tensor_tensor(sqb, ev, 0.0, mb,
                                                 op0=OP.max, op1=OP.add),
                          reads=[evk, mk], writes=[f"sq{c01}"])
                    for jj in range(4):
                        ti = hf * 8 + ts_ * 4 + jj
                        pa = pb[6 + jj % 2]
                        pak = f"pb{6 + jj % 2}"

                        def mma(e, h=h, jj=jj, pa=pa):
                            for c in range(2):
                                inst = e.matmul(
                                    pa[:, 0:HD + 1],
                                    cB[:, c * 512 + jj * 128:
                                        c * 512 + jj * 128 + 128],
                                    memz[:, (2 * h + c) * (HD + 1):
                                         (2 * h + c + 1) * (HD + 1)],
                                    start=(c == 0), stop=(c == 1))
                            return inst
                        A("tensor", mma, reads=["sq0", "sq1", "memz"],
                          writes=[pak], tag=f"am{b}.{h}.{ti}")
                        cj = 3 * (jj % 4)
                        A("vector", lambda e, pa=pa, cj=cj: e.tensor_scalar(
                            sp[:, cj:cj + 1], pa[:, HD:HD + 1], 1e-6, None,
                            op0=OP.add),
                          reads=[pak], writes=[f"rzt{jj}"])
                        A("vector", lambda e, cj=cj: e.reciprocal(
                            sp[:, cj + 1:cj + 2], sp[:, cj:cj + 1]),
                          reads=[f"rzt{jj}"], writes=[f"rz{jj}"])
                        A("vector", lambda e, cj=cj, h=h: e.tensor_scalar(
                            sp[:, cj + 2:cj + 3], sp[:, cj + 1:cj + 2],
                            gt[:, h:h + 1], None, op0=OP.mult),
                          reads=[f"rz{jj}", "gt"], writes=[f"rzg{jj}"])
                        amb = cB[:, 4096 + (jj % 2) * 256:
                                 4096 + (jj % 2) * 256 + 256]
                        A("vector", lambda e, pa=pa, amb=amb, cj=cj:
                          e.tensor_scalar(amb, pa[:, 0:HD],
                                          sp[:, cj + 2:cj + 3], None,
                                          op0=OP.mult),
                          reads=[pak, f"rzg{jj}"], writes=[f"amv{jj % 2}"])
                        A("gpsimd", lambda e, amb=amb, h=h, ti=ti:
                          e.dma_start(out=am_s[h, ti], in_=amb),
                          reads=[f"amv{jj % 2}"], writes=["amd"], is_dma=True)
            if stop_after in ("A2", "A2a", "A2aM1"):
                P.barrier()
                break
            load_whl(wk_hl, "k")
            for ot in range(NCH):
                for ts_ in range(2):
                    pq = pb[4 + (ot * 2 + ts_) % 4]
                    pqk = f"pb{4 + (ot * 2 + ts_) % 4}"
                    pairs = [(wth[:, c * 1024 + ot * 128:
                                  c * 1024 + ot * 128 + 128],
                              wtl[:, c * 1024 + ot * 128:
                                  c * 1024 + ot * 128 + 128],
                              lnTh[:, c * 1024 + ts_ * 512:
                                   c * 1024 + ts_ * 512 + 512],
                              lnTl[:, c * 1024 + ts_ * 512:
                                   c * 1024 + ts_ * 512 + 512])
                             for c in range(NCH)]
                    A("tensor", mm3(pq, pairs),
                      reads=wt_keys + lnT_keys_for(ts_),
                      writes=[pqk], tag=f"k{b}.{hf}.{ot}.{ts_}")
                    A("vector", lambda e, pq=pq, ts_=ts_: e.tensor_copy(
                        qk_h[ts_][:], pq[:, 0:512]),
                      reads=[pqk], writes=[f"qkh{ts_}"])
                    A("vector", lambda e, pq=pq, ts_=ts_: e.tensor_tensor(
                        qk_l[ts_][:], pq[:, 0:512], qk_h[ts_][:],
                        op=OP.subtract),
                      reads=[pqk, f"qkh{ts_}"], writes=[f"qkl{ts_}"])
                    for hl in range(2):
                        src = qk_h[ts_] if hl == 0 else qk_l[ts_]
                        sk = f"qk{'h' if hl == 0 else 'l'}{ts_}"
                        A("gpsimd", lambda e, src=src, hl=hl, ot=ot, hf=hf,
                          ts_=ts_: e.dma_start(
                              out=kT_s[hl, ot][:, hf * 1024 + ts_ * 512:
                                               hf * 1024 + ts_ * 512 + 512],
                              in_=src[:]),
                          reads=[sk], writes=["kTd"], is_dma=True)
            if stop_after == "A3":
                P.barrier()
                break
            load_whl(wv_hl, "v")
            for tt in range(8):
                ti = hf * 8 + tt
                for oh in range(2):
                    pq = pb[4 + (tt * 2 + oh) % 4]
                    pqk = f"pb{4 + (tt * 2 + oh) % 4}"
                    pairs = [(lnTh[:, c * 1024 + tt * 128:
                                   c * 1024 + tt * 128 + 128],
                              lnTl[:, c * 1024 + tt * 128:
                                   c * 1024 + tt * 128 + 128],
                              wth[:, c * 1024 + oh * 512:
                                  c * 1024 + oh * 512 + 512],
                              wtl[:, c * 1024 + oh * 512:
                                  c * 1024 + oh * 512 + 512])
                             for c in range(NCH)]
                    A("tensor", mm3(pq, pairs),
                      reads=wt_keys + lnT_keys_for(tt // 4),
                      writes=[pqk], tag=f"v{b}.{ti}.{oh}")
                    ev = cB[:, 2048 + oh * 512:2048 + oh * 512 + 512]
                    evk = f"ev{oh}"
                    A("scalar", lambda e, ev=ev, pq=pq: e.copy(ev, pq[:, 0:512]),
                      reads=[pqk], writes=[evk])
                    A("gpsimd", lambda e, ev=ev, ti=ti, oh=oh: e.dma_start(
                        out=v_s[ti][:, oh * 512:oh * 512 + 512], in_=ev),
                      reads=[evk], writes=["vd"], is_dma=True)
            if hf == 1:
                P.barrier()
            if stop_after and stop_after.startswith("A"):
                break

        if stop_after and stop_after.startswith("A"):
            break
        mxsh = cB[:, 5120 + 0 * 256:5120 + 512].bitcast(BF16)
        mxsl = cB[:, 5632:6144].bitcast(BF16)
        qkbufs = [
            (qhh, qhl, khh, khl),
            (cC[:, 0:2048].bitcast(BF16), cC[:, 2048:4096].bitcast(BF16),
             cC[:, 4096:6144].bitcast(BF16), cC[:, 6144:8192].bitcast(BF16)),
        ]
        for h in range(H):
            qhh_, qhl_, khh_, khl_ = qkbufs[h % 2]
            pfx = h % 2
            for c in range(2):
                for hl, (qd, kd, qb, kb) in enumerate(
                        [(qT_s, kT_s, qhh_, khh_), (qT_s, kT_s, qhl_, khl_)]):
                    A("sync", lambda e, h=h, c=c, hl=hl, qd=qd, qb=qb:
                      e.dma_start(out=qb[:, c * 2048:(c + 1) * 2048],
                                  in_=qd[hl, 2 * h + c]),
                      writes=[f"q{'h' if hl == 0 else 'l'}{c}_{pfx}"],
                      is_dma=True, tag=f"ldqh{b}.{h}")
                    A("sync", lambda e, h=h, c=c, hl=hl, kd=kd, kb=kb:
                      e.dma_start(out=kb[:, c * 2048:(c + 1) * 2048],
                                  in_=kd[hl, 2 * h + c]),
                      writes=[f"k{'h' if hl == 0 else 'l'}{c}_{pfx}"],
                      is_dma=True)
            for i in range(NT):
                A("sync", lambda e, h=h, i=i: e.dma_start(
                    out=vhb[:, i * (HD + 1):i * (HD + 1) + HD],
                    in_=v_s[i][:, h * HD:(h + 1) * HD]),
                  writes=[f"vh{i}"], is_dma=True)
            qk_keys = [f"{x}{c}_{pfx}" for x in ("qh", "ql", "kh", "kl")
                       for c in range(2)]
            if h == 0:
                pending_trmix = []
            while pending_trmix:
                pending_trmix.pop(0)()
            for J in reversed(range(4)):
                tmax = 4 * J + 4
                def _emit_score(i):
                    pass
                for i in range(tmax):
                    sbk = (0, 1, 6, 7)[i % 4]
                    pS = pb[sbk]
                    pSk = f"pb{sbk}"
                    pairs = [(khh_[:, c * 2048 + i * 128:
                                   c * 2048 + i * 128 + 128],
                              khl_[:, c * 2048 + i * 128:
                                   c * 2048 + i * 128 + 128],
                              qhh_[:, c * 2048 + J * 512:
                                   c * 2048 + J * 512 + 512],
                              qhl_[:, c * 2048 + J * 512:
                                   c * 2048 + J * 512 + 512])
                             for c in range(2)]
                    A("tensor", mm3(pS, pairs), reads=qk_keys,
                      writes=[pSk], tag=f"s{b}.{h}.{J}.{i}")
                    A("scalar", lambda e, i=i, pS=pS: e.activation(
                        expS(i * 512, (i + 1) * 512), pS[:, 0:512], AF.Exp,
                        bias=sp[:, 63:64]),
                      reads=[pSk, "eshift"], writes=[f"eS{i}"],
                      tag=f"exp{b}.{h}.{J}.{i}")
                    if i >= 4 * J:
                        dc = (i - 4 * J) * 128
                        A("vector", lambda e, i=i, dc=dc: e.tensor_tensor(
                            expS(i * 512 + dc, i * 512 + dc + 128),
                            expS(i * 512 + dc, i * 512 + dc + 128),
                            dmask[:], op=OP.mult),
                          reads=[f"eS{i}", "dmask"], writes=[f"eS{i}"],
                          tag="diag")
                while pending_trmix:
                    pending_trmix.pop(0)()
                for jj in range(4):
                    jg = 4 * J + jj
                    pA_ = pb[2 + jj % 2]
                    pAk = f"pb{2 + jj % 2}"

                    def mmpv(e, jj=jj, jg=jg, J=J, pA_=pA_):
                        for i in range(jg + 1):
                            inst = e.matmul(
                                pA_[:, 0:HD + 1],
                                expS(i * 512 + jj * 128, i * 512 + jj * 128 + 128),
                                vhb[:, i * (HD + 1):(i + 1) * (HD + 1)],
                                start=(i == 0), stop=(i == jg))
                        return inst
                    A("tensor", mmpv,
                      reads=[f"eS{i}" for i in range(jg + 1)] +
                            [f"vh{i}" for i in range(jg + 1)] +
                            [f"vones{i}" for i in range(jg + 1)],
                      writes=[pAk], tag=f"pv{b}.{h}.{jg}")
                    amb = cB[:, 4096 + (jg % 4) * 256:4096 + (jg % 4) * 256 + 256]
                    A("sync", lambda e, h=h, jg=jg, amb=amb: e.dma_start(
                        out=amb, in_=am_s[h, jg]),
                      writes=[f"amL{jg % 4}"], is_dma=True)
                    cj = 16 + 2 * (jj % 4)
                    A("vector", lambda e, pA_=pA_, cj=cj: e.reciprocal(
                        sp[:, cj:cj + 1], pA_[:, HD:HD + 1]),
                      reads=[pAk], writes=[f"rd{jj}"])
                    A("vector", lambda e, cj=cj, h=h: e.tensor_scalar(
                        sp[:, cj + 1:cj + 2], sp[:, cj:cj + 1],
                        gt[:, H + h:H + h + 1], None, op0=OP.mult),
                      reads=[f"rd{jj}", "gt1m"], writes=[f"rdm{jj}"])
                    mxb = cB[:, 10240 + jj * 256:10240 + jj * 256 + 256]
                    A("vector", lambda e, pA_=pA_, mxb=mxb, amb=amb, cj=cj:
                      e.scalar_tensor_tensor(mxb, pA_[:, 0:HD],
                                             sp[:, cj + 1:cj + 2], amb,
                                             op0=OP.mult, op1=OP.add),
                      reads=[pAk, f"rdm{jj}", f"amL{jg % 4}"],
                      writes=[f"mx{jj}"], tag=f"mix{b}.{h}.{jg}")
                def _trmix_batch(J=J, h=h):
                    for jj in range(4):
                        mxb = cB[:, 10240 + jj * 256:10240 + jj * 256 + 256]
                        for c in range(2):
                            A("tensor", lambda e, mxb=mxb, c=c: e.transpose(
                                pb[4 + c][:, 0:128],
                                mxb[:, c * 128:c * 128 + 128], idn[:]),
                              reads=[f"mx{jj}", "idn"], writes=[f"pb{4 + c}"],
                              tag="trmix")
                            A("vector", lambda e, c=c, jj=jj: e.tensor_copy(
                                mxsh[:, c * 512 + jj * 128:
                                     c * 512 + jj * 128 + 128],
                                pb[4 + c][:, 0:128]),
                              reads=[f"pb{4 + c}"], writes=[f"mxTh{c}"])
                            A("vector", lambda e, c=c, jj=jj: e.tensor_tensor(
                                mxsl[:, c * 512 + jj * 128:
                                     c * 512 + jj * 128 + 128],
                                pb[4 + c][:, 0:128],
                                mxsh[:, c * 512 + jj * 128:
                                     c * 512 + jj * 128 + 128],
                                op=OP.subtract),
                              reads=[f"pb{4 + c}", f"mxTh{c}"],
                              writes=[f"mxTl{c}"])
                    for c in range(2):
                        for hl in range(2):
                            src2 = mxsh if hl == 0 else mxsl
                            sk = f"mxT{'h' if hl == 0 else 'l'}{c}"
                            A("gpsimd", lambda e, h=h, c=c, J=J, hl=hl,
                              src2=src2: e.dma_start(
                                  out=mixT_s[hl, 2 * h + c][:,
                                             J * 512:J * 512 + 512],
                                  in_=src2[:, c * 512:c * 512 + 512]),
                              reads=[sk], writes=["mxd"], is_dma=True,
                              tag=f"stmix{b}.{h}.{J}")
                pending_trmix.append(_trmix_batch)
        while pending_trmix:
            pending_trmix.pop(0)()
        qk_alias = [f"{x}{c}_0" for x in ("qh", "ql", "kh", "kl")
                    for c in range(2)]
        for c in range(NCH):
            for hl, dst in enumerate([woh, wol]):
                A("sync", lambda e, c=c, hl=hl, dst=dst: e.dma_start(
                    out=dst[:, c * 1024:(c + 1) * 1024], in_=wo_hl[hl, c]),
                  writes=[f"wo{'h' if hl == 0 else 'l'}{c}"] + qk_alias,
                  is_dma=True, tag=f"ldwo{c}")
        wo_keys = [f"wo{x}{c}" for x in "hl" for c in range(NCH)]
        mxh0 = cA[:, 0:2048].bitcast(BF16)
        mxl0 = cA[:, 4096:6144].bitcast(BF16)
        for c in range(NCH):
            for hl, dst in enumerate([mxh0, mxl0]):
                es_alias = ([f"eS{c}"] if hl == 0 else [f"eS{8 + c}"])
                A("sync", lambda e, c=c, hl=hl, dst=dst:
                  e.dma_start(out=dst[:, c * 512:(c + 1) * 512],
                              in_=mixT_s[hl, c][:, 0:512]),
                  reads=["mxd"],
                  writes=[f"mxs0_{'h' if hl == 0 else 'l'}{c}"] + es_alias,
                  is_dma=True, tag=f"ldmx{b}.0.{c}")
        P.barrier()
        if stop_after == "B":
            break

        for sl in range(4):
            mxh = cA[:, (sl % 2) * 2048:(sl % 2) * 2048 + 2048].bitcast(BF16)
            mxl = cA[:, 4096 + (sl % 2) * 2048:
                     4096 + (sl % 2) * 2048 + 2048].bitcast(BF16)
            mxk = [f"mxs{sl % 2}_{x}{c}" for x in "hl" for c in range(NCH)]
            if sl > 0:
                for c in range(NCH):
                    for hl, dst in enumerate([mxh, mxl]):
                        A("sync", lambda e, c=c, sl=sl, hl=hl, dst=dst:
                          e.dma_start(
                              out=dst[:, c * 512:(c + 1) * 512],
                              in_=mixT_s[hl, c][:, sl * 512:sl * 512 + 512]),
                          writes=[f"mxs{sl % 2}_{'h' if hl == 0 else 'l'}{c}"],
                          is_dma=True, tag=f"ldmx{b}.{sl}.{c}")
            aoT = cB[:, 0:4096]
            x1 = cB[:, 4096:8192]
            xr_slots = [slice(4096, 5120), slice(5120, 6144),
                        slice(0, 1024), slice(1024, 2048)]
            for tt in range(4):
                ti = sl * 4 + tt
                lnb = cC[:, xr_slots[tt]]
                A("sync", lambda e, ti=ti, lnb=lnb: e.dma_start(
                    out=lnb, in_=ln1_s[ti]),
                  writes=[f"xr{tt}"], is_dma=True)
                A("scalar", lambda e, lnb=lnb, ti=ti: e.activation(
                    lnb, lnb, AF.Identity, bias=sA[:, 32 + ti:33 + ti],
                    scale=sA[:, 48 + ti:49 + ti]),
                  reads=[f"xr{tt}", "sd8", "mean8"],
                  writes=[f"xr{tt}"], tag=f"xrec{b}.{ti}")

            def _trao(ot):
                for tt in range(4):
                    ti = sl * 4 + tt
                    xr = cC[:, xr_slots[tt]]
                    pt_ = pb[2 + (ot * 4 + tt) % 4]
                    ptk = f"pb{2 + (ot * 4 + tt) % 4}"
                    A("tensor", lambda e, ot=ot, tt=tt, pt_=pt_: e.transpose(
                        pt_[:, 0:128],
                        aoT[:, ot * 512 + tt * 128:ot * 512 + tt * 128 + 128],
                        idn[:]),
                      reads=[f"aoT{ot}", "idn"], writes=[ptk], tag="trao")
                    A("vector", lambda e, ot=ot, tt=tt, pt_=pt_, xr=xr:
                      e.tensor_tensor(
                          x1[:, tt * 1024 + ot * 128:tt * 1024 + ot * 128 + 128],
                          pt_[:, 0:128], xr[:, ot * 128:ot * 128 + 128],
                          op=OP.add),
                      reads=[ptk, f"xr{tt}"], writes=[f"x1_{tt}"],
                      tag=f"x1{b}.{ti}")
            for ot in range(NCH):
                pw = pb[ot % 2]
                pwk = f"pb{ot % 2}"
                pairs = [(woh[:, c * 1024 + ot * 128:c * 1024 + ot * 128 + 128],
                          wol[:, c * 1024 + ot * 128:c * 1024 + ot * 128 + 128],
                          mxh[:, c * 512:(c + 1) * 512],
                          mxl[:, c * 512:(c + 1) * 512])
                         for c in range(NCH)]
                A("tensor", mm3(pw, pairs), reads=wo_keys + mxk, writes=[pwk],
                  tag=f"wo{b}.{sl}.{ot}")
                A("scalar", lambda e, ot=ot, pw=pw: e.activation(
                    aoT[:, ot * 512:ot * 512 + 512], pw[:, 0:512], AF.Identity,
                    bias=boT[:, ot:ot + 1]),
                  reads=[pwk, "boT"], writes=[f"aoT{ot}"])
                if ot >= 1:
                    _trao(ot - 1)
            _trao(NCH - 1)
            for tt in range(4):
                ti = sl * 4 + tt
                A("vector", lambda e, tt=tt: e.reduce_sum(
                    s2[:, tt:tt + 1], x1[:, tt * 1024:(tt + 1) * 1024],
                    axis=AX.X),
                  reads=[f"x1_{tt}"], writes=[f"s2s{tt}"])
                A("scalar", lambda e, tt=tt: e.activation(
                    trash[:], x1[:, tt * 1024:tt * 1024 + 512], AF.Square,
                    accum_out=s2[:, 4 + tt:5 + tt]),
                  reads=[f"x1_{tt}"], writes=[f"s2q{tt}", "trash"])
                A("scalar", lambda e, tt=tt: e.activation(
                    trash[:], x1[:, tt * 1024 + 512:tt * 1024 + 1024],
                    AF.Square, accum_out=s2[:, 24 + tt:25 + tt]),
                  reads=[f"x1_{tt}"], writes=[f"s2qh{tt}", "trash"])
            A("vector", lambda e: e.tensor_scalar(
                s2[:, 8:12], s2[:, 0:4], 1.0 / D, None, op0=OP.mult),
              reads=[f"s2s{t}" for t in range(4)], writes=["m2"])
            A("vector", lambda e: e.tensor_tensor(
                s2[:, 4:8], s2[:, 4:8], s2[:, 24:28], op=OP.add),
              reads=[f"s2q{t}" for t in range(4)] +
                    [f"s2qh{t}" for t in range(4)],
              writes=["s2qtot"])
            A("vector", lambda e: e.tensor_scalar(
                s2[:, 4:8], s2[:, 4:8], 1.0 / D, None, op0=OP.mult),
              reads=["s2qtot"], writes=["e2"])
            A("vector", lambda e: e.tensor_tensor(
                s2[:, 20:24], s2[:, 8:12], s2[:, 8:12], op=OP.mult),
              reads=["m2"], writes=["ms2"])
            A("vector", lambda e: e.scalar_tensor_tensor(
                s2[:, 20:24], s2[:, 4:8], eps, s2[:, 20:24],
                op0=OP.add, op1=OP.subtract),
              reads=["e2", "ms2"], writes=["ve2"])
            A("scalar", lambda e: e.activation(s2[:, 4:8], s2[:, 20:24],
                                               AF.Sqrt),
              reads=["ve2"], writes=["sd2"])
            A("vector", lambda e: e.reciprocal(s2[:, 12:16], s2[:, 4:8]),
              reads=["sd2"], writes=["r2"])
            ln2Th = cB[:, 8192:10240].bitcast(BF16)
            ln2Tl = cB[:, 10240:12288].bitcast(BF16)
            for tt in range(4):
                lnb = cC[:, 2048 + (tt % 2) * 1024:2048 + (tt % 2) * 1024 + 1024]
                A("vector", lambda e, tt=tt, lnb=lnb: e.tensor_scalar(
                    lnb, x1[:, tt * 1024:(tt + 1) * 1024],
                    s2[:, 8 + tt:9 + tt], s2[:, 12 + tt:13 + tt],
                    op0=OP.subtract, op1=OP.mult),
                  reads=[f"x1_{tt}", "m2", "r2"], writes=[f"ln2_{tt % 2}"],
                  tag=f"ln2{b}.{sl}.{tt}")
                for c in range(NCH):
                    pt_ = pb[6 + c % 2]
                    ptk = f"pb{6 + c % 2}"
                    A("tensor", lambda e, lnb=lnb, c=c, pt_=pt_: e.transpose(
                        pt_[:, 0:128], lnb[:, c * 128:c * 128 + 128], idn[:]),
                      reads=[f"ln2_{tt % 2}", "idn"], writes=[ptk],
                      tag="trln2")
                    A("scalar", lambda e, c=c, tt=tt, pt_=pt_: e.copy(
                        ln2Th[:, c * 512 + tt * 128:c * 512 + tt * 128 + 128],
                        pt_[:, 0:128]),
                      reads=[ptk], writes=[f"ln2Th{c}"])
                    A("vector", lambda e, c=c, tt=tt, pt_=pt_: e.tensor_tensor(
                        ln2Tl[:, c * 512 + tt * 128:c * 512 + tt * 128 + 128],
                        pt_[:, 0:128],
                        ln2Th[:, c * 512 + tt * 128:c * 512 + tt * 128 + 128],
                        op=OP.subtract),
                      reads=[ptk, f"ln2Th{c}"], writes=[f"ln2Tl{c}"])
            ln2T_keys = [f"ln2T{x}{c}" for x in "hl" for c in range(NCH)]
            ffnT = aoT
            for ot in range(NCH):
                pw = pb[ot % 2]
                pwk = f"pb{ot % 2}"
                pairs = [(mTh[:, c * D + ot * 128:c * D + ot * 128 + 128],
                          mTl[:, c * D + ot * 128:c * D + ot * 128 + 128],
                          ln2Th[:, c * 512:(c + 1) * 512],
                          ln2Tl[:, c * 512:(c + 1) * 512])
                         for c in range(NCH)]
                A("tensor", mm3(pw, pairs), reads=ln2T_keys, writes=[pwk],
                  tag=f"ffn{b}.{sl}.{ot}")
                A("scalar", lambda e, ot=ot, pw=pw: e.activation(
                    ffnT[:, ot * 512:ot * 512 + 512], pw[:, 0:512], AF.Identity,
                    bias=beT[:, ot:ot + 1]),
                  reads=[pwk, "beT"], writes=[f"ffnT{ot}"])
            for tt in range(4):
                ti = sl * 4 + tt
                ob = cC[:, 6144 + (tt % 2) * 1024:6144 + (tt % 2) * 1024 + 1024]
                for ot in range(NCH):
                    pt_ = pb[2 + ot % 4]
                    ptk = f"pb{2 + ot % 4}"
                    A("tensor", lambda e, ot=ot, tt=tt, pt_=pt_: e.transpose(
                        pt_[:, 0:128],
                        ffnT[:, ot * 512 + tt * 128:ot * 512 + tt * 128 + 128],
                        idn[:]),
                      reads=[f"ffnT{ot}", "idn"], writes=[ptk], tag="trffn")
                    A("vector", lambda e, ot=ot, tt=tt, pt_=pt_, ob=ob:
                      e.tensor_tensor(
                          ob[:, ot * 128:ot * 128 + 128], pt_[:, 0:128],
                          x1[:, tt * 1024 + ot * 128:tt * 1024 + ot * 128 + 128],
                          op=OP.add),
                      reads=[ptk, f"x1_{tt}"], writes=[f"ob{tt % 2}"],
                      tag=f"out{b}.{ti}")
                A("gpsimd", lambda e, ob=ob, b=b, ti=ti: e.dma_start(
                    out=y_r[b, ti], in_=ob),
                  reads=[f"ob{tt % 2}"], writes=[f"yd{b}_{ti}"], is_dma=True,
                  tag=f"sty{b}.{ti}")
        if b == 0:
            for i in range(2):
                A("sync", lambda e, i=i: e.dma_start(
                    out=cB[:, 6144 + i * 1024:6144 + (i + 1) * 1024],
                    in_=x_r[1, i]),
                  writes=[f"xs{i}", f"x1_{2 + i}"], is_dma=True,
                  tag=f"hoistx1.{i}")
        wo_alias = [f"wo{x}{c}" for x in "hl" for c in range(NCH)]
        for c in range(NCH):
            A("sync", lambda e, c=c: e.dma_start(
                out=wth[:, c * 1024:(c + 1) * 1024], in_=wq_hl[0, c]),
              writes=[f"wth{c}"] + (wo_alias if c == 0 else []), is_dma=True,
              tag=f"hoistwq{c}")
            A("sync", lambda e, c=c: e.dma_start(
                out=wtl[:, c * 1024:(c + 1) * 1024], in_=wq_hl[1, c]),
              writes=[f"wtl{c}"], is_dma=True)
        P.barrier()

    if stop_after is None:
        out_keys = [f"yd{b}_{ti}" for b in range(B) for ti in range(NT)]
        A("gpsimd", lambda e: None, reads=out_keys, tag="final_wait")

    with nc.Block() as block:
        P.emit(block, sems)
    es.close()
    return nc


_NC_CACHE = None


def kernel(L0_attn_hidden_states, L0_attn_Wq, L0_attn_Wk, L0_attn_Wv,
           L0_attn_Wo, L0_attn_bo, L0_attn_betas, L0_attn_mem, L0_attn_z,
           L0_ffn_W1, L0_ffn_b1, L0_ffn_W2, L0_ffn_b2):
    global _NC_CACHE
    f = np.float32
    x = np.ascontiguousarray(L0_attn_hidden_states, dtype=f)
    wqt = np.ascontiguousarray(np.asarray(L0_attn_Wq, dtype=f).T)
    wkt = np.ascontiguousarray(np.asarray(L0_attn_Wk, dtype=f).T)
    wvt = np.ascontiguousarray(np.asarray(L0_attn_Wv, dtype=f).T)
    wot = np.ascontiguousarray(np.asarray(L0_attn_Wo, dtype=f).T)
    bo = np.ascontiguousarray(L0_attn_bo, dtype=f)
    betas = np.ascontiguousarray(np.asarray(L0_attn_betas, dtype=f)
                                 .reshape(1, 4))
    memz = np.ascontiguousarray(np.concatenate(
        [np.asarray(L0_attn_mem, dtype=f),
         np.asarray(L0_attn_z, dtype=f)], axis=2))
    w1 = np.asarray(L0_ffn_W1, dtype=f)
    b1 = np.asarray(L0_ffn_b1, dtype=f)
    w2 = np.asarray(L0_ffn_W2, dtype=f)
    b2 = np.asarray(L0_ffn_b2, dtype=f)
    idn = np.eye(128, dtype=f)
    dmask = np.triu(np.ones((128, 128), dtype=f))

    if _NC_CACHE is None:
        _NC_CACHE = _build_nc()
    nc = _NC_CACHE

    in_maps = []
    for c in range(8):
        in_maps.append({
            "x": np.ascontiguousarray(x[2 * c:2 * c + 2]),
            "wqt": wqt, "wkt": wkt, "wvt": wvt, "wot": wot,
            "bo": bo, "betas": betas, "memz": memz,
            "w1": np.ascontiguousarray(w1[c]),
            "w2t": np.ascontiguousarray(w2[c].T),
            "b1": np.ascontiguousarray(b1[c]),
            "b2": np.ascontiguousarray(b2[c]),
            "idn": idn, "dmask": dmask,
        })
    res = run_bass_kernel_spmd(nc, in_maps, core_ids=list(range(8)))
    out = np.concatenate([np.asarray(r["y"]) for r in res.results], axis=0)
    return out.astype(np.float32)


# version 19
# speedup vs baseline: 1.1115x; 1.0815x over previous
import numpy as np
import concourse.bass as bass
import concourse.mybir as mybir
from concourse.bass_utils import run_bass_kernel_spmd

F32 = mybir.dt.float32
BF16 = mybir.dt.bfloat16
AF = mybir.ActivationFunctionType
OP = mybir.AluOpType
AX = mybir.AxisListType

ENGINES = ("sync", "scalar", "vector", "tensor", "gpsimd")


DMA_SLOTS = 8


class _Op:
    __slots__ = ("idx", "engine", "fn", "reads", "writes", "is_dma",
                 "token_needed", "deps", "tag", "sem_name")

    def __init__(self, idx, engine, fn, reads, writes, is_dma, tag=""):
        self.idx = idx
        self.engine = engine
        self.fn = fn
        self.reads = tuple(reads)
        self.writes = tuple(writes)
        self.is_dma = is_dma
        self.token_needed = False
        self.deps = []
        self.tag = tag
        self.sem_name = engine


class _Prog:

    def __init__(self):
        self.ops = []
        self.last_writer = {}
        self.readers = {}
        self.last_on = {}
        self.dma_n = {}
        self.slot_last = {}

    def add(self, engine, fn, reads=(), writes=(), is_dma=False, tag=""):
        op = _Op(len(self.ops), engine, fn, reads, writes, is_dma, tag)
        deps = {}
        for k in op.reads:
            w = self.last_writer.get(k)
            if w is not None:
                deps[w.idx] = w
            if k.startswith("pb") and engine in ("scalar", "vector"):
                for r in self.readers.get(k, ()):
                    if r.engine in ("scalar", "vector") and \
                            r.engine != engine:
                        deps[r.idx] = r
            self.readers.setdefault(k, []).append(op)
        for k in op.writes:
            w = self.last_writer.get(k)
            if w is not None:
                deps[w.idx] = w
            for r in self.readers.get(k, ()):
                if r.engine != op.engine:
                    deps[r.idx] = r
        for k in op.writes:
            self.last_writer[k] = op
            self.readers[k] = []
        if is_dma:
            n = self.dma_n.get(engine, 0)
            self.dma_n[engine] = n + 1
            slot = n % DMA_SLOTS
            op.sem_name = f"{engine}_d{slot}"
            prev = self.slot_last.get(op.sem_name)
            if prev is not None:
                deps[prev.idx] = prev
            self.slot_last[op.sem_name] = op
        deps.pop(op.idx, None)
        op.deps = list(deps.values())
        self.ops.append(op)
        self.last_on[engine] = op
        return op

    def barrier(self):
        lasts = list(self.last_on.values()) + list(self.slot_last.values())
        for e in ENGINES:
            deps = [o for o in lasts if o is not self.last_on.get(e)]
            if not deps:
                continue
            op = _Op(len(self.ops), e, lambda eng: None, (), (), False,
                     tag="barrier")
            op.deps = deps
            self.ops.append(op)
            self.last_on[e] = op
        self.last_writer.clear()
        self.readers.clear()

    def sem_names(self):
        names = set(ENGINES)
        for q in ("sync", "gpsimd", "scalar", "vector"):
            for s in range(DMA_SLOTS):
                names.add(f"{q}_d{s}")
        return sorted(names)

    def emit(self, block, sems):
        for op in self.ops:
            if op.is_dma:
                op.token_needed = True
            for d in op.deps:
                d.token_needed = True
        counts = {}
        tokval = {}
        for op in self.ops:
            if op.token_needed:
                c = counts.get(op.sem_name, 0) + (16 if op.is_dma else 1)
                counts[op.sem_name] = c
                tokval[op.idx] = c
        per_engine = {e: [] for e in ENGINES}
        for op in self.ops:
            per_engine[op.engine].append(op)

        def mk(ename):
            ops_list = per_engine[ename]

            def closure(eng):
                waited = {}
                for op in ops_list:
                    for d in op.deps:
                        v = tokval[d.idx]
                        if waited.get(d.sem_name, -1) < v:
                            eng.wait_ge(sems[d.sem_name], v)
                            waited[d.sem_name] = v
                    inst = op.fn(eng)
                    if op.token_needed:
                        inst.then_inc(sems[op.sem_name],
                                      16 if op.is_dma else 1)
            return closure

        for e in ENGINES:
            if per_engine[e]:
                getattr(block, e)(mk(e))


def _build_nc(stop_after=None):
    B, S, D, H, HD = 2, 2048, 1024, 4, 256
    NT = S // 128
    NCH = D // 128
    INTER = 4096

    nc = bass.Bass("TRN2", target_bir_lowering=False)

    def dram(n, sh, kind=None, dt=F32):
        t = (nc.dram_tensor(n, sh, dt) if kind is None
             else nc.dram_tensor(n, sh, dt, kind=kind))
        return t.ap()

    x_in = dram("x", [B, S, D], "ExternalInput")
    wqt = dram("wqt", [D, D], "ExternalInput")
    wkt = dram("wkt", [D, D], "ExternalInput")
    wvt = dram("wvt", [D, D], "ExternalInput")
    wot = dram("wot", [D, D], "ExternalInput")
    bo_in = dram("bo", [D], "ExternalInput")
    beta_in = dram("betas", [1, H], "ExternalInput")
    memz_in = dram("memz", [H, HD, HD + 1], "ExternalInput")
    w1_in = dram("w1", [INTER, D], "ExternalInput")
    w2t_in = dram("w2t", [INTER, D], "ExternalInput")
    b1_in = dram("b1", [INTER], "ExternalInput")
    b2_in = dram("b2", [D], "ExternalInput")
    idn_in = dram("idn", [128, 128], "ExternalInput")
    dmask_in = dram("dmask", [128, 128], "ExternalInput")
    y_out = dram("y", [B, S, D], "ExternalOutput")

    qT_s = dram("qT_s", [2, NCH, 128, S], dt=BF16)
    kT_s = dram("kT_s", [2, NCH, 128, S], dt=BF16)
    v_s = dram("v_s", [NT, 128, D])
    am_s = dram("am_s", [H, NT, 128, HD])
    ln1_s = dram("ln1_s", [NT, 128, D])
    mixT_s = dram("mixT_s", [2, NCH, 128, S], dt=BF16)
    be_s = dram("be_s", [1, D])
    wq_hl = dram("wq_hl", [2, NCH, 128, D], dt=BF16)
    wk_hl = dram("wk_hl", [2, NCH, 128, D], dt=BF16)
    wv_hl = dram("wv_hl", [2, NCH, 128, D], dt=BF16)
    wo_hl = dram("wo_hl", [2, NCH, 128, D], dt=BF16)

    x_r = x_in.rearrange("b (t p) d -> b t p d", p=128)
    y_r = y_out.rearrange("b (t p) d -> b t p d", p=128)
    wqt_r = wqt.rearrange("(c p) o -> c p o", p=128)
    wkt_r = wkt.rearrange("(c p) o -> c p o", p=128)
    wvt_r = wvt.rearrange("(c p) o -> c p o", p=128)
    wot_r = wot.rearrange("(c p) o -> c p o", p=128)
    bo_r = bo_in.rearrange("(c p) -> p c", p=128)
    b2_r = b2_in.rearrange("(c p) -> p c", p=128)
    b1_r = b1_in.rearrange("(c p) -> p c", p=128)
    be_r = be_s.rearrange("a (c p) -> (a p) c", p=128)
    memz_r = memz_in.rearrange("h (c p) e -> h c p e", p=128)
    w1_r = w1_in.rearrange("(c p) d -> c p d", p=128)
    w2t_r = w2t_in.rearrange("(c p) d -> c p d", p=128)

    from contextlib import ExitStack
    es = ExitStack()
    sb = lambda n, w: es.enter_context(nc.sbuf_tensor(n, [128, w], F32))
    mT = sb("mT", NCH * D)
    vhb = sb("vhb", NT * (HD + 1))

    class _Split:
        def __init__(self, parts, w=4096):
            self.parts = parts
            self.w = w

        def __getitem__(self, key):
            ps, cs = key
            lo, hi = cs.start, cs.stop
            i = lo // self.w
            assert hi <= (i + 1) * self.w, (lo, hi)
            return self.parts[i][ps, lo - i * self.w:hi - i * self.w]

    cA = _Split([sb(f"cA{i}", 4096) for i in range(4)])
    cB = _Split([sb(f"cB{i}", 4096) for i in range(3)])
    cC = _Split([sb(f"cC{i}", 4096) for i in range(2)])
    idn = sb("idn_t", 128)
    dmask = sb("dmask_t", 128)
    memz = sb("memz_t", 2 * H * (HD + 1))
    boT = sb("boT", NCH)
    beT = sb("beT", NCH)
    b2T = sb("b2T", NCH)
    b1s = sb("b1s", INTER // 128)
    gt = sb("gt", 2 * H)
    sA = sb("sA", 64)
    s2 = sb("s2", 32)
    sp = sb("sp", 64)
    trash = sb("trash", 512)
    pb = [es.enter_context(nc.psum_tensor(f"pb{i}", [128, 512], F32))
          for i in range(8)]

    mTh = mT[:, 0:4096].bitcast(BF16)
    mTl = mT[:, 4096:8192].bitcast(BF16)
    lnTh = cA[:, 0:4096].bitcast(BF16)
    lnTl = cA[:, 4096:8192].bitcast(BF16)
    wth = cA[:, 8192:12288].bitcast(BF16)
    wtl = cA[:, 12288:16384].bitcast(BF16)
    def expS(lo, hi):
        return cA[:, lo:hi]
    qhh = cA[:, 8192:10240].bitcast(BF16)
    qhl = cA[:, 10240:12288].bitcast(BF16)
    khh = cA[:, 12288:14336].bitcast(BF16)
    khl = cA[:, 14336:16384].bitcast(BF16)
    woh = cA[:, 8192:12288].bitcast(BF16)
    wol = cA[:, 12288:16384].bitcast(BF16)
    ones1 = cB[0:1, 1028:1156]
    brow = cB[0:1, 0:1024]
    betar = cB[0:1, 1024:1028]

    P = _Prog()
    sems = {}
    for name in P.sem_names():
        sems[name] = es.enter_context(nc.semaphore(f"s_{name}"))
    A = P.add

    A("sync", lambda e: e.dma_start(out=idn[:], in_=idn_in[:]),
      writes=["idn"], is_dma=True, tag="ld_idn")
    A("sync", lambda e: e.dma_start(out=dmask[:], in_=dmask_in[:]),
      writes=["dmask"], is_dma=True, tag="ld_dmask")
    for h in range(H):
        for c in range(2):
            A("sync", lambda e, h=h, c=c: e.dma_start(
                out=memz[:, (2 * h + c) * (HD + 1):(2 * h + c + 1) * (HD + 1)],
                in_=memz_r[h, c]),
              writes=["memz"], is_dma=True, tag="ld_memz")
    A("sync", lambda e: e.dma_start(allow_slow_non_contiguous=True,
                                    out=boT[:], in_=bo_r[:]),
      writes=["boT"], is_dma=True)
    A("sync", lambda e: e.dma_start(allow_slow_non_contiguous=True,
                                    out=b2T[:], in_=b2_r[:]),
      writes=["b2T"], is_dma=True)
    A("sync", lambda e: e.dma_start(allow_slow_non_contiguous=True,
                                    out=b1s[:], in_=b1_r[:]),
      writes=["b1s"], is_dma=True)
    A("sync", lambda e: e.dma_start(out=betar[:], in_=beta_in[:]),
      writes=["betar"], is_dma=True)
    for i in range(NT):
        A("gpsimd", lambda e, i=i: e.memset(
            vhb[:, i * (HD + 1) + HD:(i + 1) * (HD + 1)], 1.0),
          writes=[f"vones{i}"])
    A("gpsimd", lambda e: e.memset(ones1[:], 1.0), writes=["ones1"])
    A("gpsimd", lambda e: e.memset(sp[:, 63:64], -20.0), writes=["eshift"])

    A("scalar", lambda e: e.activation(betar[:], betar[:], AF.Sigmoid),
      reads=["betar"], writes=["betar"])
    A("tensor", lambda e: e.matmul(pb[0][:, 0:H], ones1[:], betar[:],
                                   start=True, stop=True),
      reads=["betar", "ones1"], writes=["pb0"], tag="gbc")
    A("vector", lambda e: e.tensor_copy(gt[:, 0:H], pb[0][:, 0:H]),
      reads=["pb0"], writes=["gt"])
    A("vector", lambda e: e.tensor_scalar(gt[:, H:2 * H], gt[:, 0:H],
                                          -1.0, 1.0, op0=OP.mult, op1=OP.add),
      reads=["gt"], writes=["gt1m"])

    NI = INTER // 128
    wsin = [cC[:, 5120:6144], cC[:, 5120:6144]]
    wsh = cC[:, 7168:7680].bitcast(BF16)
    wsl = cC[:, 7680:8192].bitcast(BF16)

    def _wsplit_steps():
        for wi, (wt_r, whl) in enumerate([(wqt_r, wq_hl), (wkt_r, wk_hl),
                                          (wvt_r, wv_hl), (wot_r, wo_hl)]):
            for c in range(NCH):
                yield (wi, c, wt_r, whl)

    def _emit_wsplit(step):
        wi, c, wt_r, whl = step
        k = 0
        buf = wsin[k]
        A("sync", lambda e, buf=buf, wt_r=wt_r, c=c: e.dma_start(
            out=buf, in_=wt_r[c]),
          writes=[f"wsin{k}"], is_dma=True, tag=f"ldws{wi}.{c}")
        A("scalar", lambda e, buf=buf: e.copy(wsh[:], buf),
          reads=[f"wsin{k}"], writes=["wsh"], tag="wsplit_h")
        A("vector", lambda e, buf=buf: e.tensor_tensor(
            wsl[:], buf, wsh[:], op=OP.subtract),
          reads=[f"wsin{k}", "wsh"], writes=["wsl"], tag="wsplit_l")
        A("gpsimd", lambda e, whl=whl, c=c: e.dma_start(
            out=whl[0, c], in_=wsh[:]),
          reads=["wsh"], writes=["whld"], is_dma=True)
        A("gpsimd", lambda e, whl=whl, c=c: e.dma_start(
            out=whl[1, c], in_=wsl[:]),
          reads=["wsl"], writes=["whld"], is_dma=True)

    _wsteps = iter(_wsplit_steps())
    for c in range(NI):
        buf = cC[:, (c % 2) * 1024:(c % 2) * 1024 + 1024]
        A("sync", lambda e, c=c, buf=buf: e.dma_start(out=buf, in_=w2t_r[c]),
          writes=[f"w2tc{c % 2}"], is_dma=True, tag=f"ld_w2t{c}")
        for half in range(2):
            A("tensor", lambda e, c=c, buf=buf, half=half: e.matmul(
                pb[6 + half][0:1, 0:512], b1s[:, c:c + 1],
                buf[:, half * 512:half * 512 + 512],
                start=(c == 0), stop=(c == NI - 1)),
              reads=[f"w2tc{c % 2}", "b1s"], writes=[f"pb{6 + half}"],
              tag=f"be{c}.{half}")
    for half in range(2):
        A("vector", lambda e, half=half: e.tensor_copy(
            brow[:, half * 512:half * 512 + 512], pb[6 + half][0:1, 0:512]),
          reads=[f"pb{6 + half}"], writes=["brow"])
    A("gpsimd", lambda e: e.dma_start(out=be_s[:], in_=brow[:]),
      reads=["brow"], writes=["be_d"], is_dma=True)
    A("sync", lambda e: e.dma_start(allow_slow_non_contiguous=True,
                                    out=beT[:], in_=be_r[:]),
      reads=["be_d"], writes=["beT"], is_dma=True)
    A("vector", lambda e: e.tensor_tensor(beT[:], beT[:], b2T[:], op=OP.add),
      reads=["beT", "b2T"], writes=["beT"])

    w1hb = [cC[:, 0 + i * 512:512 + i * 512].bitcast(BF16) for i in range(2)]
    w1lb = [cC[:, 1024 + i * 512:1536 + i * 512].bitcast(BF16)
            for i in range(2)]
    w2hb = [cC[:, 6144 + i * 256:6400 + i * 256].bitcast(BF16)
            for i in range(2)]
    w2lb = [cC[:, 6656 + i * 256:6912 + i * 256].bitcast(BF16)
            for i in range(2)]
    for osl in range(2):
        for c in range(NI):
            s = c % 2
            w1b = cC[:, 2048 + s * 1024:2048 + s * 1024 + 1024]
            w2b = cC[:, 4096 + s * 512:4096 + s * 512 + 512]
            A("sync", lambda e, c=c, w1b=w1b: e.dma_start(out=w1b,
                                                          in_=w1_r[c]),
              writes=[f"w1c{s}", "w2tc0" if s == 0 else "w2tc1"],
              is_dma=True, tag=f"ld_w1_{osl}_{c}")
            A("sync", lambda e, c=c, w2b=w2b, osl=osl: e.dma_start(
                out=w2b, in_=w2t_r[c][:, osl * 512:osl * 512 + 512]),
              writes=[f"w2c{s}"], is_dma=True)
            A("scalar", lambda e, s=s, w1b=w1b: e.copy(w1hb[s][:], w1b),
              reads=[f"w1c{s}"], writes=[f"w1h{s}", "w2tc0" if s == 0
                                         else "w2tc1"])
            A("vector", lambda e, s=s, w1b=w1b: e.tensor_tensor(
                w1lb[s][:], w1b, w1hb[s][:], op=OP.subtract),
              reads=[f"w1c{s}", f"w1h{s}"], writes=[f"w1l{s}"])
            A("scalar", lambda e, s=s, w2b=w2b: e.copy(w2hb[s][:], w2b),
              reads=[f"w2c{s}"], writes=[f"w2h{s}"])
            A("vector", lambda e, s=s, w2b=w2b: e.tensor_tensor(
                w2lb[s][:], w2b, w2hb[s][:], op=OP.subtract),
              reads=[f"w2c{s}", f"w2h{s}"], writes=[f"w2l{s}"])

            def mmg(e, c=c, s=s):
                n = 0
                for p in range(3):
                    a_ = w1hb[s] if p < 2 else w1lb[s]
                    b_ = w2hb[s] if p != 1 else w2lb[s]
                    for it in range(8):
                        inst = e.matmul(pb[it][:, 0:512],
                                        a_[:, it * 128:it * 128 + 128], b_[:],
                                        start=(c == 0 and p == 0),
                                        stop=(c == NI - 1 and p == 2))
                        n += 1
                return inst
            A("tensor", mmg,
              reads=[f"w1h{s}", f"w1l{s}", f"w2h{s}", f"w2l{s}"],
              writes=[f"pb{it}" for it in range(8)], tag=f"mT{osl}.{c}")
            ws = next(_wsteps, None)
            if ws is not None:
                _emit_wsplit(ws)
        for it in range(8):
            A("scalar", lambda e, it=it, osl=osl: e.copy(
                mTh[:, it * D + osl * 512:it * D + osl * 512 + 512],
                pb[it][:, 0:512]),
              reads=[f"pb{it}"], writes=[f"mTh{it}"], tag=f"evmTh{osl}.{it}")
            A("vector", lambda e, it=it, osl=osl: e.tensor_tensor(
                mTl[:, it * D + osl * 512:it * D + osl * 512 + 512],
                pb[it][:, 0:512],
                mTh[:, it * D + osl * 512:it * D + osl * 512 + 512],
                op=OP.subtract),
              reads=[f"pb{it}", f"mTh{it}"], writes=[f"mTl{it}"],
              tag=f"evmTl{osl}.{it}")
    for ws in _wsteps:
        _emit_wsplit(ws)
    for i in range(2):
        A("sync", lambda e, i=i: e.dma_start(
            out=cB[:, 6144 + i * 1024:6144 + (i + 1) * 1024], in_=x_r[0, i]),
          writes=[f"xs{i}"], is_dma=True, tag=f"hoistx0.{i}")
    for c in range(NCH):
        A("sync", lambda e, c=c: e.dma_start(
            out=wth[:, c * 1024:(c + 1) * 1024], in_=wq_hl[0, c]),
          writes=[f"wth{c}"], is_dma=True, tag=f"hoistwq0.{c}")
        A("sync", lambda e, c=c: e.dma_start(
            out=wtl[:, c * 1024:(c + 1) * 1024], in_=wq_hl[1, c]),
          writes=[f"wtl{c}"], is_dma=True)
    P.barrier()

    def mm3(pq, pairs, n_free=512):
        def fn(e):
            n = len(pairs) * 3
            i = 0
            for p in range(3):
                for (lh, ll, rh, rl) in pairs:
                    a = lh if p < 2 else ll
                    bb = rh if p != 1 else rl
                    inst = e.matmul(pq[:, 0:n_free], a, bb,
                                    start=(i == 0), stop=(i == n - 1))
                    i += 1
            return inst
        return fn

    eps = 1e-5
    if stop_after == "p0":
        with nc.Block() as block:
            P.emit(block, sems)
        es.close()
        return nc
    for b in range(B):
        for hf in range(2):
            xst = [cB[:, 6144 + i * 1024:6144 + (i + 1) * 1024]
                   for i in range(2)]
            for tt in range(8):
                ti = hf * 8 + tt
                xb = xst[tt % 2]
                if not (hf == 0 and tt < 2):
                    A("sync", lambda e, b=b, ti=ti, xb=xb: e.dma_start(
                        out=xb, in_=x_r[b, ti]),
                      writes=[f"xs{tt % 2}"], is_dma=True, tag=f"ldx{b}.{ti}")
                A("vector", lambda e, xb=xb, ti=ti: e.reduce_sum(
                    sA[:, ti:ti + 1], xb, axis=AX.X),
                  reads=[f"xs{tt % 2}"], writes=[f"sum{ti}"])
                A("scalar", lambda e, xb=xb, ti=ti: e.activation(
                    trash[:], xb[:, 0:512], AF.Square,
                    accum_out=sA[:, 16 + ti:17 + ti]),
                  reads=[f"xs{tt % 2}"], writes=[f"ssq{ti}", "trash"])
                A("scalar", lambda e, xb=xb, tt=tt: e.activation(
                    trash[:], xb[:, 512:1024], AF.Square,
                    accum_out=s2[:, 16 + tt:17 + tt]),
                  reads=[f"xs{tt % 2}"], writes=[f"ssqh{tt}", "trash"])
            sl8 = slice(hf * 8, hf * 8 + 8)
            A("vector", lambda e, sl8=sl8, hf=hf: e.tensor_scalar(
                sA[:, 32 + hf * 8:40 + hf * 8], sA[:, sl8], 1.0 / D, None,
                op0=OP.mult),
              reads=[f"sum{hf * 8 + i}" for i in range(8)],
              writes=["mean8"], tag="mean")
            A("vector", lambda e, hf=hf: e.tensor_tensor(
                sA[:, 16 + hf * 8:24 + hf * 8], sA[:, 16 + hf * 8:24 + hf * 8],
                s2[:, 16:24], op=OP.add),
              reads=[f"ssq{hf * 8 + i}" for i in range(8)] +
                    [f"ssqh{i}" for i in range(8)],
              writes=["ssqtot"])
            A("vector", lambda e, hf=hf: e.tensor_scalar(
                sA[:, 16 + hf * 8:24 + hf * 8], sA[:, 16 + hf * 8:24 + hf * 8],
                1.0 / D, None, op0=OP.mult),
              reads=["ssqtot"], writes=["ex28"])
            A("vector", lambda e, hf=hf: e.tensor_tensor(
                s2[:, 24:32], sA[:, 32 + hf * 8:40 + hf * 8],
                sA[:, 32 + hf * 8:40 + hf * 8], op=OP.mult),
              reads=["mean8"], writes=["msq8"])
            A("vector", lambda e, hf=hf: e.scalar_tensor_tensor(
                s2[:, 24:32], sA[:, 16 + hf * 8:24 + hf * 8], eps,
                s2[:, 24:32], op0=OP.add, op1=OP.subtract),
              reads=["ex28", "msq8"], writes=["veps8"])
            A("scalar", lambda e, hf=hf: e.activation(
                sA[:, 48 + hf * 8:56 + hf * 8], s2[:, 24:32], AF.Sqrt),
              reads=["veps8"], writes=["sd8"], tag="sd")
            A("vector", lambda e, hf=hf: e.reciprocal(
                sA[:, 16 + hf * 8:24 + hf * 8], sA[:, 48 + hf * 8:56 + hf * 8]),
              reads=["sd8"], writes=["rstd8"])
            for tt in range(8):
                ti = hf * 8 + tt
                xb = xst[tt % 2]
                lnb = cB[:, 8192 + (tt % 2) * 1024:8192 + (tt % 2 + 1) * 1024]
                A("sync", lambda e, b=b, ti=ti, xb=xb: e.dma_start(
                    out=xb, in_=x_r[b, ti]),
                  writes=[f"xs{tt % 2}"], is_dma=True)
                A("vector", lambda e, xb=xb, lnb=lnb, ti=ti: e.tensor_scalar(
                    lnb, xb, sA[:, 32 + ti:33 + ti], sA[:, 16 + ti:17 + ti],
                    op0=OP.subtract, op1=OP.mult),
                  reads=[f"xs{tt % 2}", "mean8", "rstd8"],
                  writes=[f"ln1t{tt % 2}"], tag=f"ln1_{b}.{ti}")
                A("gpsimd", lambda e, lnb=lnb, ti=ti: e.dma_start(
                    out=ln1_s[ti], in_=lnb),
                  reads=[f"ln1t{tt % 2}"], writes=[f"ln1d{ti}"], is_dma=True)
                for c in range(NCH):
                    pbk = c % 4
                    dst = tt * 128
                    A("tensor", lambda e, lnb=lnb, c=c, pbk=pbk: e.transpose(
                        pb[pbk][:, 0:128], lnb[:, c * 128:c * 128 + 128],
                        idn[:]),
                      reads=[f"ln1t{tt % 2}", "idn"], writes=[f"pb{pbk}"],
                      tag=f"trln{ti}.{c}")
                    hg = tt // 4
                    A("scalar", lambda e, c=c, dst=dst, pbk=pbk: e.copy(
                        lnTh[:, c * 1024 + dst:c * 1024 + dst + 128],
                        pb[pbk][:, 0:128]),
                      reads=[f"pb{pbk}"], writes=[f"lnTh{c}t{hg}"])
                    A("vector", lambda e, c=c, dst=dst, pbk=pbk:
                      e.tensor_tensor(
                          lnTl[:, c * 1024 + dst:c * 1024 + dst + 128],
                          pb[pbk][:, 0:128],
                          lnTh[:, c * 1024 + dst:c * 1024 + dst + 128],
                          op=OP.subtract),
                      reads=[f"pb{pbk}", f"lnTh{c}t{hg}"],
                      writes=[f"lnTl{c}t{hg}"])
            def lnT_keys_for(ts2):
                return [f"lnTh{c}t{ts2}" for c in range(NCH)] + \
                       [f"lnTl{c}t{ts2}" for c in range(NCH)]
            lnT_keys = lnT_keys_for(0) + lnT_keys_for(1)
            qk_h = [cB[:, 3072 + i * 256:3072 + i * 256 + 256].bitcast(BF16)
                    for i in range(2)]
            qk_l = [cB[:, 3584 + i * 256:3584 + i * 256 + 256].bitcast(BF16)
                    for i in range(2)]
            if stop_after == "A1":
                P.barrier()
                break
            if stop_after in ("A1w", "A1m", "A1e", "A1f", "A1g", "A1h"):
                for c in range(NCH):
                    A("sync", lambda e, c=c: e.dma_start(
                        out=wth[:, c * 1024:(c + 1) * 1024], in_=wq_hl[0, c]),
                      writes=[f"wth{c}"], is_dma=True)
                    A("sync", lambda e, c=c: e.dma_start(
                        out=wtl[:, c * 1024:(c + 1) * 1024], in_=wq_hl[1, c]),
                      writes=[f"wtl{c}"], is_dma=True)
                if stop_after in ("A1m", "A1e"):
                    wk = [f"wth{c}" for c in range(NCH)] + \
                         [f"wtl{c}" for c in range(NCH)]
                    lk = [f"lnTh{c}" for c in range(NCH)] + \
                         [f"lnTl{c}" for c in range(NCH)]
                    for h in range(H):
                        for ts_ in range(2):
                            for c01 in range(2):
                                ot = 2 * h + c01
                                pq = pb[4 + c01]
                                pqk = f"pb{4 + c01}"
                                pairs = [(wth[:, c * 1024 + ot * 128:
                                              c * 1024 + ot * 128 + 128],
                                          wtl[:, c * 1024 + ot * 128:
                                              c * 1024 + ot * 128 + 128],
                                          lnTh[:, c * 1024 + ts_ * 512:
                                               c * 1024 + ts_ * 512 + 512],
                                          lnTl[:, c * 1024 + ts_ * 512:
                                               c * 1024 + ts_ * 512 + 512])
                                         for c in range(NCH)]
                                A("tensor", mm3(pq, pairs), reads=wk + lk,
                                  writes=[pqk], tag=f"qm{ot}.{ts_}")
                                ev = cB[:, 2048 + c01 * 512:
                                        2048 + c01 * 512 + 512]
                                A("vector", lambda e, ev=ev, pq=pq:
                                  e.tensor_copy(ev, pq[:, 0:512]),
                                  reads=[pqk], writes=[f"ev{c01}"])
                                if stop_after in ("A1e", "A1f", "A1g", "A1h"):
                                    A("vector", lambda e, pq=pq, c01=c01:
                                      e.tensor_copy(qk_h[c01][:], pq[:, 0:512]),
                                      reads=[pqk], writes=[f"qkh{c01}"])
                                    A("vector", lambda e, pq=pq, c01=c01:
                                      e.tensor_tensor(qk_l[c01][:],
                                                      pq[:, 0:512],
                                                      qk_h[c01][:],
                                                      op=OP.subtract),
                                      reads=[pqk, f"qkl{c01}"],
                                      writes=[f"qkl{c01}"])
                                if stop_after in ("A1f", "A1g", "A1h"):
                                    mb = cB[:, 1024 + c01 * 512:
                                            1024 + c01 * 512 + 512]
                                    mk = f"qm{c01}"
                                    A("vector", lambda e, ev=ev, mb=mb:
                                      e.tensor_scalar(mb, ev, 0.0, None,
                                                      op0=OP.min),
                                      reads=[f"ev{c01}"], writes=[mk])
                                    A("scalar", lambda e, mb=mb: e.activation(
                                        mb, mb, AF.Exp),
                                      reads=[mk], writes=[mk])
                                    sqb = cB[:, c01 * 512:c01 * 512 + 512]
                                    A("vector", lambda e, ev=ev, mb=mb,
                                      sqb=sqb: e.scalar_tensor_tensor(
                                          sqb, ev, 0.0, mb, op0=OP.max,
                                          op1=OP.add),
                                      reads=[f"ev{c01}", mk],
                                      writes=[f"sq{c01}"])
                            if stop_after in ("A1g", "A1h"):
                                for jj in range(4):
                                    pa = pb[6 + jj % 2]
                                    pak = f"pb{6 + jj % 2}"

                                    def mma2(e, h=h, jj=jj, pa=pa):
                                        for c in range(2):
                                            inst = e.matmul(
                                                pa[:, 0:HD + 1],
                                                cB[:, c * 512 + jj * 128:
                                                    c * 512 + jj * 128 + 128],
                                                memz[:, (2 * h + c) * (HD + 1):
                                                     (2 * h + c + 1) * (HD + 1)],
                                                start=(c == 0), stop=(c == 1))
                                        return inst
                                    A("tensor", mma2,
                                      reads=["sq0", "sq1", "memz"],
                                      writes=[pak])
                                    amb = cB[:, 4096 + (jj % 2) * 256:
                                             4096 + (jj % 2) * 256 + 256]
                                    A("vector", lambda e, pa=pa, amb=amb:
                                      e.tensor_scalar(amb, pa[:, 0:HD], 1.0,
                                                      None, op0=OP.mult),
                                      reads=[pak], writes=[f"amv{jj % 2}"])
                                    if stop_after in ("A1h",):
                                        ti2 = hf * 8 + ts_ * 4 + jj
                                        A("gpsimd", lambda e, amb=amb, h=h,
                                          ti2=ti2: e.dma_start(
                                              out=am_s[h, ti2], in_=amb),
                                          reads=[f"amv{jj % 2}"],
                                          writes=["amd"], is_dma=True)
                P.barrier()
                break

            def load_whl(whl, name):
                for c in range(NCH):
                    A("sync", lambda e, c=c, whl=whl: e.dma_start(
                        out=wth[:, c * 1024:(c + 1) * 1024], in_=whl[0, c]),
                      writes=[f"wth{c}"], is_dma=True, tag=f"ld{name}h{c}")
                    A("sync", lambda e, c=c, whl=whl: e.dma_start(
                        out=wtl[:, c * 1024:(c + 1) * 1024], in_=whl[1, c]),
                      writes=[f"wtl{c}"], is_dma=True, tag=f"ld{name}l{c}")
            wt_keys = [f"wth{c}" for c in range(NCH)] + \
                      [f"wtl{c}" for c in range(NCH)]

            if hf == 1:
                load_whl(wq_hl, "q")
            for h in range(H):
                for ts_ in range(2):
                    for c01 in range(2):
                        ot = 2 * h + c01
                        pq = pb[4 + c01]
                        pqk = f"pb{4 + c01}"
                        pairs = [(wth[:, c * 1024 + ot * 128:
                                      c * 1024 + ot * 128 + 128],
                                  wtl[:, c * 1024 + ot * 128:
                                      c * 1024 + ot * 128 + 128],
                                  lnTh[:, c * 1024 + ts_ * 512:
                                       c * 1024 + ts_ * 512 + 512],
                                  lnTl[:, c * 1024 + ts_ * 512:
                                       c * 1024 + ts_ * 512 + 512])
                                 for c in range(NCH)]
                        A("tensor", mm3(pq, pairs),
                          reads=wt_keys + lnT_keys_for(ts_),
                          writes=[pqk], tag=f"q{b}.{hf}.{ot}.{ts_}")
                        ev = cB[:, 2048 + c01 * 512:2048 + c01 * 512 + 512]
                        evk = f"ev{c01}"
                        A("scalar", lambda e, ev=ev, pq=pq: e.copy(
                            ev, pq[:, 0:512]),
                          reads=[pqk], writes=[evk])
                        A("vector", lambda e, pq=pq, c01=c01: e.tensor_copy(
                            qk_h[c01][:], pq[:, 0:512]),
                          reads=[pqk], writes=[f"qkh{c01}"])
                        A("vector", lambda e, pq=pq, c01=c01: e.tensor_tensor(
                            qk_l[c01][:], pq[:, 0:512], qk_h[c01][:],
                            op=OP.subtract),
                          reads=[pqk, f"qkh{c01}"], writes=[f"qkl{c01}"])
                        for hl in range(2):
                            if stop_after in ("A2a", "A2aM1"):
                                break
                            src = qk_h[c01] if hl == 0 else qk_l[c01]
                            sk = f"qk{'h' if hl == 0 else 'l'}{c01}"
                            A("gpsimd", lambda e, src=src, hl=hl, ot=ot,
                              hf=hf, ts_=ts_: e.dma_start(
                                  out=qT_s[hl, ot][:, hf * 1024 + ts_ * 512:
                                                   hf * 1024 + ts_ * 512 + 512],
                                  in_=src[:]),
                              reads=[sk], writes=["qTd"], is_dma=True)
                        mb = cB[:, 1024 + c01 * 512:1024 + c01 * 512 + 512]
                        mk = f"qm{c01}"
                        A("vector", lambda e, ev=ev, mb=mb: e.tensor_scalar(
                            mb, ev, 0.0, None, op0=OP.min),
                          reads=[evk], writes=[mk])
                        A("scalar", lambda e, mb=mb: e.activation(
                            mb, mb, AF.Exp),
                          reads=[mk], writes=[mk], tag="eluexp")
                        sqb = cB[:, c01 * 512:c01 * 512 + 512]
                        A("vector", lambda e, ev=ev, mb=mb, sqb=sqb:
                          e.scalar_tensor_tensor(sqb, ev, 0.0, mb,
                                                 op0=OP.max, op1=OP.add),
                          reads=[evk, mk], writes=[f"sq{c01}"])
                    for jj in range(4):
                        ti = hf * 8 + ts_ * 4 + jj
                        pa = pb[6 + jj % 2]
                        pak = f"pb{6 + jj % 2}"

                        def mma(e, h=h, jj=jj, pa=pa):
                            for c in range(2):
                                inst = e.matmul(
                                    pa[:, 0:HD + 1],
                                    cB[:, c * 512 + jj * 128:
                                        c * 512 + jj * 128 + 128],
                                    memz[:, (2 * h + c) * (HD + 1):
                                         (2 * h + c + 1) * (HD + 1)],
                                    start=(c == 0), stop=(c == 1))
                            return inst
                        A("tensor", mma, reads=["sq0", "sq1", "memz"],
                          writes=[pak], tag=f"am{b}.{h}.{ti}")
                        cj = 3 * (jj % 4)
                        A("vector", lambda e, pa=pa, cj=cj: e.tensor_scalar(
                            sp[:, cj:cj + 1], pa[:, HD:HD + 1], 1e-6, None,
                            op0=OP.add),
                          reads=[pak], writes=[f"rzt{jj}"])
                        A("vector", lambda e, cj=cj: e.reciprocal(
                            sp[:, cj + 1:cj + 2], sp[:, cj:cj + 1]),
                          reads=[f"rzt{jj}"], writes=[f"rz{jj}"])
                        A("vector", lambda e, cj=cj, h=h: e.tensor_scalar(
                            sp[:, cj + 2:cj + 3], sp[:, cj + 1:cj + 2],
                            gt[:, h:h + 1], None, op0=OP.mult),
                          reads=[f"rz{jj}", "gt"], writes=[f"rzg{jj}"])
                        amb = cB[:, 4096 + (jj % 2) * 256:
                                 4096 + (jj % 2) * 256 + 256]
                        A("vector", lambda e, pa=pa, amb=amb, cj=cj:
                          e.tensor_scalar(amb, pa[:, 0:HD],
                                          sp[:, cj + 2:cj + 3], None,
                                          op0=OP.mult),
                          reads=[pak, f"rzg{jj}"], writes=[f"amv{jj % 2}"])
                        A("gpsimd", lambda e, amb=amb, h=h, ti=ti:
                          e.dma_start(out=am_s[h, ti], in_=amb),
                          reads=[f"amv{jj % 2}"], writes=["amd"], is_dma=True)
            if stop_after in ("A2", "A2a", "A2aM1"):
                P.barrier()
                break
            load_whl(wk_hl, "k")
            for ot in range(NCH):
                for ts_ in range(2):
                    pq = pb[4 + (ot * 2 + ts_) % 4]
                    pqk = f"pb{4 + (ot * 2 + ts_) % 4}"
                    pairs = [(wth[:, c * 1024 + ot * 128:
                                  c * 1024 + ot * 128 + 128],
                              wtl[:, c * 1024 + ot * 128:
                                  c * 1024 + ot * 128 + 128],
                              lnTh[:, c * 1024 + ts_ * 512:
                                   c * 1024 + ts_ * 512 + 512],
                              lnTl[:, c * 1024 + ts_ * 512:
                                   c * 1024 + ts_ * 512 + 512])
                             for c in range(NCH)]
                    A("tensor", mm3(pq, pairs),
                      reads=wt_keys + lnT_keys_for(ts_),
                      writes=[pqk], tag=f"k{b}.{hf}.{ot}.{ts_}")
                    A("vector", lambda e, pq=pq, ts_=ts_: e.tensor_copy(
                        qk_h[ts_][:], pq[:, 0:512]),
                      reads=[pqk], writes=[f"qkh{ts_}"])
                    A("vector", lambda e, pq=pq, ts_=ts_: e.tensor_tensor(
                        qk_l[ts_][:], pq[:, 0:512], qk_h[ts_][:],
                        op=OP.subtract),
                      reads=[pqk, f"qkh{ts_}"], writes=[f"qkl{ts_}"])
                    for hl in range(2):
                        src = qk_h[ts_] if hl == 0 else qk_l[ts_]
                        sk = f"qk{'h' if hl == 0 else 'l'}{ts_}"
                        A("gpsimd", lambda e, src=src, hl=hl, ot=ot, hf=hf,
                          ts_=ts_: e.dma_start(
                              out=kT_s[hl, ot][:, hf * 1024 + ts_ * 512:
                                               hf * 1024 + ts_ * 512 + 512],
                              in_=src[:]),
                          reads=[sk], writes=["kTd"], is_dma=True)
            if stop_after == "A3":
                P.barrier()
                break
            load_whl(wv_hl, "v")
            for tt in range(8):
                ti = hf * 8 + tt
                for oh in range(2):
                    pq = pb[4 + (tt * 2 + oh) % 4]
                    pqk = f"pb{4 + (tt * 2 + oh) % 4}"
                    pairs = [(lnTh[:, c * 1024 + tt * 128:
                                   c * 1024 + tt * 128 + 128],
                              lnTl[:, c * 1024 + tt * 128:
                                   c * 1024 + tt * 128 + 128],
                              wth[:, c * 1024 + oh * 512:
                                  c * 1024 + oh * 512 + 512],
                              wtl[:, c * 1024 + oh * 512:
                                  c * 1024 + oh * 512 + 512])
                             for c in range(NCH)]
                    A("tensor", mm3(pq, pairs),
                      reads=wt_keys + lnT_keys_for(tt // 4),
                      writes=[pqk], tag=f"v{b}.{ti}.{oh}")
                    ev = cB[:, 2048 + oh * 512:2048 + oh * 512 + 512]
                    evk = f"ev{oh}"
                    A("scalar", lambda e, ev=ev, pq=pq: e.copy(ev, pq[:, 0:512]),
                      reads=[pqk], writes=[evk])
                    A("gpsimd", lambda e, ev=ev, ti=ti, oh=oh: e.dma_start(
                        out=v_s[ti][:, oh * 512:oh * 512 + 512], in_=ev),
                      reads=[evk], writes=["vd"], is_dma=True)
            if hf == 1:
                P.barrier()
            if stop_after and stop_after.startswith("A"):
                break

        if stop_after and stop_after.startswith("A"):
            break
        mxsh = cB[:, 5120 + 0 * 256:5120 + 512].bitcast(BF16)
        mxsl = cB[:, 5632:6144].bitcast(BF16)
        qkbufs = [
            (qhh, qhl, khh, khl),
            (cC[:, 0:2048].bitcast(BF16), cC[:, 2048:4096].bitcast(BF16),
             cC[:, 4096:6144].bitcast(BF16), cC[:, 6144:8192].bitcast(BF16)),
        ]
        for h in range(H):
            qhh_, qhl_, khh_, khl_ = qkbufs[h % 2]
            pfx = h % 2
            for c in range(2):
                for hl, (qd, kd, qb, kb) in enumerate(
                        [(qT_s, kT_s, qhh_, khh_), (qT_s, kT_s, qhl_, khl_)]):
                    A("sync", lambda e, h=h, c=c, hl=hl, qd=qd, qb=qb:
                      e.dma_start(out=qb[:, c * 2048:(c + 1) * 2048],
                                  in_=qd[hl, 2 * h + c]),
                      writes=[f"q{'h' if hl == 0 else 'l'}{c}_{pfx}"],
                      is_dma=True, tag=f"ldqh{b}.{h}")
                    A("sync", lambda e, h=h, c=c, hl=hl, kd=kd, kb=kb:
                      e.dma_start(out=kb[:, c * 2048:(c + 1) * 2048],
                                  in_=kd[hl, 2 * h + c]),
                      writes=[f"k{'h' if hl == 0 else 'l'}{c}_{pfx}"],
                      is_dma=True)
            for i in range(NT):
                A("sync", lambda e, h=h, i=i: e.dma_start(
                    out=vhb[:, i * (HD + 1):i * (HD + 1) + HD],
                    in_=v_s[i][:, h * HD:(h + 1) * HD]),
                  writes=[f"vh{i}"], is_dma=True)
            qk_keys = [f"{x}{c}_{pfx}" for x in ("qh", "ql", "kh", "kl")
                       for c in range(2)]
            if h == 0:
                pending_trmix = []
            while pending_trmix:
                pending_trmix.pop(0)()
            for J in reversed(range(4)):
                tmax = 4 * J + 4
                def _emit_score(i):
                    pass
                for i in range(tmax):
                    sbk = (0, 1, 6, 7)[i % 4]
                    pS = pb[sbk]
                    pSk = f"pb{sbk}"
                    pairs = [(khh_[:, c * 2048 + i * 128:
                                   c * 2048 + i * 128 + 128],
                              khl_[:, c * 2048 + i * 128:
                                   c * 2048 + i * 128 + 128],
                              qhh_[:, c * 2048 + J * 512:
                                   c * 2048 + J * 512 + 512],
                              qhl_[:, c * 2048 + J * 512:
                                   c * 2048 + J * 512 + 512])
                             for c in range(2)]
                    A("tensor", mm3(pS, pairs), reads=qk_keys,
                      writes=[pSk], tag=f"s{b}.{h}.{J}.{i}")
                    A("scalar", lambda e, i=i, pS=pS: e.activation(
                        expS(i * 512, (i + 1) * 512), pS[:, 0:512], AF.Exp,
                        bias=sp[:, 63:64]),
                      reads=[pSk, "eshift"], writes=[f"eS{i}"],
                      tag=f"exp{b}.{h}.{J}.{i}")
                    if i >= 4 * J:
                        dc = (i - 4 * J) * 128
                        A("vector", lambda e, i=i, dc=dc: e.tensor_tensor(
                            expS(i * 512 + dc, i * 512 + dc + 128),
                            expS(i * 512 + dc, i * 512 + dc + 128),
                            dmask[:], op=OP.mult),
                          reads=[f"eS{i}", "dmask"], writes=[f"eS{i}"],
                          tag="diag")
                while pending_trmix:
                    pending_trmix.pop(0)()
                for jj in range(4):
                    jg = 4 * J + jj
                    pA_ = pb[2 + jj % 2]
                    pAk = f"pb{2 + jj % 2}"

                    def mmpv(e, jj=jj, jg=jg, J=J, pA_=pA_):
                        for i in range(jg + 1):
                            inst = e.matmul(
                                pA_[:, 0:HD + 1],
                                expS(i * 512 + jj * 128, i * 512 + jj * 128 + 128),
                                vhb[:, i * (HD + 1):(i + 1) * (HD + 1)],
                                start=(i == 0), stop=(i == jg))
                        return inst
                    A("tensor", mmpv,
                      reads=[f"eS{i}" for i in range(jg + 1)] +
                            [f"vh{i}" for i in range(jg + 1)] +
                            [f"vones{i}" for i in range(jg + 1)],
                      writes=[pAk], tag=f"pv{b}.{h}.{jg}")
                    amb = cB[:, 4096 + (jg % 4) * 256:4096 + (jg % 4) * 256 + 256]
                    A("sync", lambda e, h=h, jg=jg, amb=amb: e.dma_start(
                        out=amb, in_=am_s[h, jg]),
                      writes=[f"amL{jg % 4}"], is_dma=True)
                    cj = 16 + 2 * (jj % 4)
                    A("vector", lambda e, pA_=pA_, cj=cj: e.reciprocal(
                        sp[:, cj:cj + 1], pA_[:, HD:HD + 1]),
                      reads=[pAk], writes=[f"rd{jj}"])
                    A("vector", lambda e, cj=cj, h=h: e.tensor_scalar(
                        sp[:, cj + 1:cj + 2], sp[:, cj:cj + 1],
                        gt[:, H + h:H + h + 1], None, op0=OP.mult),
                      reads=[f"rd{jj}", "gt1m"], writes=[f"rdm{jj}"])
                    mxb = cB[:, 10240 + jj * 256:10240 + jj * 256 + 256]
                    A("vector", lambda e, pA_=pA_, mxb=mxb, amb=amb, cj=cj:
                      e.scalar_tensor_tensor(mxb, pA_[:, 0:HD],
                                             sp[:, cj + 1:cj + 2], amb,
                                             op0=OP.mult, op1=OP.add),
                      reads=[pAk, f"rdm{jj}", f"amL{jg % 4}"],
                      writes=[f"mx{jj}"], tag=f"mix{b}.{h}.{jg}")
                def _trmix_batch(J=J, h=h):
                    for jj in range(4):
                        mxb = cB[:, 10240 + jj * 256:10240 + jj * 256 + 256]
                        for c in range(2):
                            A("tensor", lambda e, mxb=mxb, c=c: e.transpose(
                                pb[4 + c][:, 0:128],
                                mxb[:, c * 128:c * 128 + 128], idn[:]),
                              reads=[f"mx{jj}", "idn"], writes=[f"pb{4 + c}"],
                              tag="trmix")
                            A("vector", lambda e, c=c, jj=jj: e.tensor_copy(
                                mxsh[:, c * 512 + jj * 128:
                                     c * 512 + jj * 128 + 128],
                                pb[4 + c][:, 0:128]),
                              reads=[f"pb{4 + c}"], writes=[f"mxTh{c}"])
                            A("vector", lambda e, c=c, jj=jj: e.tensor_tensor(
                                mxsl[:, c * 512 + jj * 128:
                                     c * 512 + jj * 128 + 128],
                                pb[4 + c][:, 0:128],
                                mxsh[:, c * 512 + jj * 128:
                                     c * 512 + jj * 128 + 128],
                                op=OP.subtract),
                              reads=[f"pb{4 + c}", f"mxTh{c}"],
                              writes=[f"mxTl{c}"])
                    for c in range(2):
                        for hl in range(2):
                            src2 = mxsh if hl == 0 else mxsl
                            sk = f"mxT{'h' if hl == 0 else 'l'}{c}"
                            A("gpsimd", lambda e, h=h, c=c, J=J, hl=hl,
                              src2=src2: e.dma_start(
                                  out=mixT_s[hl, 2 * h + c][:,
                                             J * 512:J * 512 + 512],
                                  in_=src2[:, c * 512:c * 512 + 512]),
                              reads=[sk], writes=["mxd"], is_dma=True,
                              tag=f"stmix{b}.{h}.{J}")
                pending_trmix.append(_trmix_batch)
        while pending_trmix:
            pending_trmix.pop(0)()
        qk_alias = [f"{x}{c}_0" for x in ("qh", "ql", "kh", "kl")
                    for c in range(2)]
        for c in range(NCH):
            for hl, dst in enumerate([woh, wol]):
                A("sync", lambda e, c=c, hl=hl, dst=dst: e.dma_start(
                    out=dst[:, c * 1024:(c + 1) * 1024], in_=wo_hl[hl, c]),
                  writes=[f"wo{'h' if hl == 0 else 'l'}{c}"] + qk_alias,
                  is_dma=True, tag=f"ldwo{c}")
        wo_keys = [f"wo{x}{c}" for x in "hl" for c in range(NCH)]
        mxh0 = cA[:, 0:2048].bitcast(BF16)
        mxl0 = cA[:, 4096:6144].bitcast(BF16)
        for c in range(NCH):
            for hl, dst in enumerate([mxh0, mxl0]):
                es_alias = ([f"eS{c}"] if hl == 0 else [f"eS{8 + c}"])
                A("sync", lambda e, c=c, hl=hl, dst=dst:
                  e.dma_start(out=dst[:, c * 512:(c + 1) * 512],
                              in_=mixT_s[hl, c][:, 0:512]),
                  reads=["mxd"],
                  writes=[f"mxs0_{'h' if hl == 0 else 'l'}{c}"] + es_alias,
                  is_dma=True, tag=f"ldmx{b}.0.{c}")
        P.barrier()
        if stop_after == "B":
            break

        for sl in range(4):
            mxh = cA[:, (sl % 2) * 2048:(sl % 2) * 2048 + 2048].bitcast(BF16)
            mxl = cA[:, 4096 + (sl % 2) * 2048:
                     4096 + (sl % 2) * 2048 + 2048].bitcast(BF16)
            mxk = [f"mxs{sl % 2}_{x}{c}" for x in "hl" for c in range(NCH)]
            if sl > 0:
                for c in range(NCH):
                    for hl, dst in enumerate([mxh, mxl]):
                        A("sync", lambda e, c=c, sl=sl, hl=hl, dst=dst:
                          e.dma_start(
                              out=dst[:, c * 512:(c + 1) * 512],
                              in_=mixT_s[hl, c][:, sl * 512:sl * 512 + 512]),
                          writes=[f"mxs{sl % 2}_{'h' if hl == 0 else 'l'}{c}"],
                          is_dma=True, tag=f"ldmx{b}.{sl}.{c}")
            aoT = cB[:, 0:4096]
            x1 = cB[:, 4096:8192]
            xr_slots = [slice(4096, 5120), slice(5120, 6144),
                        slice(0, 1024), slice(1024, 2048)]
            for tt in range(4):
                ti = sl * 4 + tt
                lnb = cC[:, xr_slots[tt]]
                A("sync", lambda e, ti=ti, lnb=lnb: e.dma_start(
                    out=lnb, in_=ln1_s[ti]),
                  writes=[f"xr{tt}"] + ([f"ob{tt + 2}"] if tt < 2 else []),
                  is_dma=True)
                A("scalar", lambda e, lnb=lnb, ti=ti: e.activation(
                    lnb, lnb, AF.Identity, bias=sA[:, 32 + ti:33 + ti],
                    scale=sA[:, 48 + ti:49 + ti]),
                  reads=[f"xr{tt}", "sd8", "mean8"],
                  writes=[f"xr{tt}"], tag=f"xrec{b}.{ti}")

            def _trao(ot):
                for tt in range(4):
                    ti = sl * 4 + tt
                    xr = cC[:, xr_slots[tt]]
                    pt_ = pb[2 + (ot * 4 + tt) % 4]
                    ptk = f"pb{2 + (ot * 4 + tt) % 4}"
                    A("tensor", lambda e, ot=ot, tt=tt, pt_=pt_: e.transpose(
                        pt_[:, 0:128],
                        aoT[:, ot * 512 + tt * 128:ot * 512 + tt * 128 + 128],
                        idn[:]),
                      reads=[f"aoT{ot}", "idn"], writes=[ptk], tag="trao")
                    A("vector", lambda e, ot=ot, tt=tt, pt_=pt_, xr=xr:
                      e.tensor_tensor(
                          x1[:, tt * 1024 + ot * 128:tt * 1024 + ot * 128 + 128],
                          pt_[:, 0:128], xr[:, ot * 128:ot * 128 + 128],
                          op=OP.add),
                      reads=[ptk, f"xr{tt}"], writes=[f"x1_{tt}"],
                      tag=f"x1{b}.{ti}")
            for ot in range(NCH):
                pw = pb[ot % 2]
                pwk = f"pb{ot % 2}"
                pairs = [(woh[:, c * 1024 + ot * 128:c * 1024 + ot * 128 + 128],
                          wol[:, c * 1024 + ot * 128:c * 1024 + ot * 128 + 128],
                          mxh[:, c * 512:(c + 1) * 512],
                          mxl[:, c * 512:(c + 1) * 512])
                         for c in range(NCH)]
                A("tensor", mm3(pw, pairs), reads=wo_keys + mxk, writes=[pwk],
                  tag=f"wo{b}.{sl}.{ot}")
                A("scalar", lambda e, ot=ot, pw=pw: e.activation(
                    aoT[:, ot * 512:ot * 512 + 512], pw[:, 0:512], AF.Identity,
                    bias=boT[:, ot:ot + 1]),
                  reads=[pwk, "boT"], writes=[f"aoT{ot}"])
                if ot >= 1:
                    _trao(ot - 1)
            _trao(NCH - 1)
            for tt in range(4):
                ti = sl * 4 + tt
                A("vector", lambda e, tt=tt: e.reduce_sum(
                    s2[:, tt:tt + 1], x1[:, tt * 1024:(tt + 1) * 1024],
                    axis=AX.X),
                  reads=[f"x1_{tt}"], writes=[f"s2s{tt}"])
                A("scalar", lambda e, tt=tt: e.activation(
                    trash[:], x1[:, tt * 1024:tt * 1024 + 512], AF.Square,
                    accum_out=s2[:, 4 + tt:5 + tt]),
                  reads=[f"x1_{tt}"], writes=[f"s2q{tt}", "trash"])
                A("scalar", lambda e, tt=tt: e.activation(
                    trash[:], x1[:, tt * 1024 + 512:tt * 1024 + 1024],
                    AF.Square, accum_out=s2[:, 24 + tt:25 + tt]),
                  reads=[f"x1_{tt}"], writes=[f"s2qh{tt}", "trash"])
            A("vector", lambda e: e.tensor_scalar(
                s2[:, 8:12], s2[:, 0:4], 1.0 / D, None, op0=OP.mult),
              reads=[f"s2s{t}" for t in range(4)], writes=["m2"])
            A("vector", lambda e: e.tensor_tensor(
                s2[:, 4:8], s2[:, 4:8], s2[:, 24:28], op=OP.add),
              reads=[f"s2q{t}" for t in range(4)] +
                    [f"s2qh{t}" for t in range(4)],
              writes=["s2qtot"])
            A("vector", lambda e: e.tensor_scalar(
                s2[:, 4:8], s2[:, 4:8], 1.0 / D, None, op0=OP.mult),
              reads=["s2qtot"], writes=["e2"])
            A("vector", lambda e: e.tensor_tensor(
                s2[:, 20:24], s2[:, 8:12], s2[:, 8:12], op=OP.mult),
              reads=["m2"], writes=["ms2"])
            A("vector", lambda e: e.scalar_tensor_tensor(
                s2[:, 20:24], s2[:, 4:8], eps, s2[:, 20:24],
                op0=OP.add, op1=OP.subtract),
              reads=["e2", "ms2"], writes=["ve2"])
            A("scalar", lambda e: e.activation(s2[:, 4:8], s2[:, 20:24],
                                               AF.Sqrt),
              reads=["ve2"], writes=["sd2"])
            A("vector", lambda e: e.reciprocal(s2[:, 12:16], s2[:, 4:8]),
              reads=["sd2"], writes=["r2"])
            ln2Th = cB[:, 8192:10240].bitcast(BF16)
            ln2Tl = cB[:, 10240:12288].bitcast(BF16)
            for tt in range(4):
                lnb = cC[:, 2048 + (tt % 2) * 1024:2048 + (tt % 2) * 1024 + 1024]
                A("vector", lambda e, tt=tt, lnb=lnb: e.tensor_scalar(
                    lnb, x1[:, tt * 1024:(tt + 1) * 1024],
                    s2[:, 8 + tt:9 + tt], s2[:, 12 + tt:13 + tt],
                    op0=OP.subtract, op1=OP.mult),
                  reads=[f"x1_{tt}", "m2", "r2"], writes=[f"ln2_{tt % 2}"],
                  tag=f"ln2{b}.{sl}.{tt}")
                for c in range(NCH):
                    pt_ = pb[6 + c % 2]
                    ptk = f"pb{6 + c % 2}"
                    A("tensor", lambda e, lnb=lnb, c=c, pt_=pt_: e.transpose(
                        pt_[:, 0:128], lnb[:, c * 128:c * 128 + 128], idn[:]),
                      reads=[f"ln2_{tt % 2}", "idn"], writes=[ptk],
                      tag="trln2")
                    A("scalar", lambda e, c=c, tt=tt, pt_=pt_: e.copy(
                        ln2Th[:, c * 512 + tt * 128:c * 512 + tt * 128 + 128],
                        pt_[:, 0:128]),
                      reads=[ptk], writes=[f"ln2Th{c}"])
                    A("vector", lambda e, c=c, tt=tt, pt_=pt_: e.tensor_tensor(
                        ln2Tl[:, c * 512 + tt * 128:c * 512 + tt * 128 + 128],
                        pt_[:, 0:128],
                        ln2Th[:, c * 512 + tt * 128:c * 512 + tt * 128 + 128],
                        op=OP.subtract),
                      reads=[ptk, f"ln2Th{c}"], writes=[f"ln2Tl{c}"])
            ln2T_keys = [f"ln2T{x}{c}" for x in "hl" for c in range(NCH)]
            ffnT = aoT
            ob_slots = [slice(6144, 7168), slice(7168, 8192),
                        slice(4096, 5120), slice(5120, 6144)]

            def _trffn(ot):
                for tt in range(4):
                    ob = cC[:, ob_slots[tt]]
                    pt_ = pb[2 + (ot * 4 + tt) % 4]
                    ptk = f"pb{2 + (ot * 4 + tt) % 4}"
                    A("tensor", lambda e, ot=ot, tt=tt, pt_=pt_: e.transpose(
                        pt_[:, 0:128],
                        ffnT[:, ot * 512 + tt * 128:ot * 512 + tt * 128 + 128],
                        idn[:]),
                      reads=[f"ffnT{ot}", "idn"], writes=[ptk], tag="trffn")
                    aliases = [f"xr{tt - 2}"] if (ot == 0 and tt >= 2) else []
                    A("vector", lambda e, ot=ot, tt=tt, pt_=pt_, ob=ob:
                      e.tensor_tensor(
                          ob[:, ot * 128:ot * 128 + 128], pt_[:, 0:128],
                          x1[:, tt * 1024 + ot * 128:tt * 1024 + ot * 128 + 128],
                          op=OP.add),
                      reads=[ptk, f"x1_{tt}"], writes=[f"ob{tt}"] + aliases,
                      tag=f"out{b}.{sl * 4 + tt}")
            for ot in range(NCH):
                pw = pb[ot % 2]
                pwk = f"pb{ot % 2}"
                pairs = [(mTh[:, c * D + ot * 128:c * D + ot * 128 + 128],
                          mTl[:, c * D + ot * 128:c * D + ot * 128 + 128],
                          ln2Th[:, c * 512:(c + 1) * 512],
                          ln2Tl[:, c * 512:(c + 1) * 512])
                         for c in range(NCH)]
                A("tensor", mm3(pw, pairs), reads=ln2T_keys, writes=[pwk],
                  tag=f"ffn{b}.{sl}.{ot}")
                A("scalar", lambda e, ot=ot, pw=pw: e.activation(
                    ffnT[:, ot * 512:ot * 512 + 512], pw[:, 0:512], AF.Identity,
                    bias=beT[:, ot:ot + 1]),
                  reads=[pwk, "beT"], writes=[f"ffnT{ot}"])
                if ot >= 1:
                    _trffn(ot - 1)
            _trffn(NCH - 1)
            for tt in range(4):
                ti = sl * 4 + tt
                ob = cC[:, ob_slots[tt]]
                A("gpsimd", lambda e, ob=ob, b=b, ti=ti: e.dma_start(
                    out=y_r[b, ti], in_=ob),
                  reads=[f"ob{tt}"], writes=[f"yd{b}_{ti}"], is_dma=True,
                  tag=f"sty{b}.{ti}")
        if b == 0:
            for i in range(2):
                A("sync", lambda e, i=i: e.dma_start(
                    out=cB[:, 6144 + i * 1024:6144 + (i + 1) * 1024],
                    in_=x_r[1, i]),
                  writes=[f"xs{i}", f"x1_{2 + i}"], is_dma=True,
                  tag=f"hoistx1.{i}")
        wo_alias = [f"wo{x}{c}" for x in "hl" for c in range(NCH)]
        for c in range(NCH):
            A("sync", lambda e, c=c: e.dma_start(
                out=wth[:, c * 1024:(c + 1) * 1024], in_=wq_hl[0, c]),
              writes=[f"wth{c}"] + (wo_alias if c == 0 else []), is_dma=True,
              tag=f"hoistwq{c}")
            A("sync", lambda e, c=c: e.dma_start(
                out=wtl[:, c * 1024:(c + 1) * 1024], in_=wq_hl[1, c]),
              writes=[f"wtl{c}"], is_dma=True)
        P.barrier()

    if stop_after is None:
        out_keys = [f"yd{b}_{ti}" for b in range(B) for ti in range(NT)]
        A("gpsimd", lambda e: None, reads=out_keys, tag="final_wait")

    with nc.Block() as block:
        P.emit(block, sems)
    es.close()
    return nc


_NC_CACHE = None


def kernel(L0_attn_hidden_states, L0_attn_Wq, L0_attn_Wk, L0_attn_Wv,
           L0_attn_Wo, L0_attn_bo, L0_attn_betas, L0_attn_mem, L0_attn_z,
           L0_ffn_W1, L0_ffn_b1, L0_ffn_W2, L0_ffn_b2):
    global _NC_CACHE
    f = np.float32
    x = np.ascontiguousarray(L0_attn_hidden_states, dtype=f)
    wqt = np.ascontiguousarray(np.asarray(L0_attn_Wq, dtype=f).T)
    wkt = np.ascontiguousarray(np.asarray(L0_attn_Wk, dtype=f).T)
    wvt = np.ascontiguousarray(np.asarray(L0_attn_Wv, dtype=f).T)
    wot = np.ascontiguousarray(np.asarray(L0_attn_Wo, dtype=f).T)
    bo = np.ascontiguousarray(L0_attn_bo, dtype=f)
    betas = np.ascontiguousarray(np.asarray(L0_attn_betas, dtype=f)
                                 .reshape(1, 4))
    memz = np.ascontiguousarray(np.concatenate(
        [np.asarray(L0_attn_mem, dtype=f),
         np.asarray(L0_attn_z, dtype=f)], axis=2))
    w1 = np.asarray(L0_ffn_W1, dtype=f)
    b1 = np.asarray(L0_ffn_b1, dtype=f)
    w2 = np.asarray(L0_ffn_W2, dtype=f)
    b2 = np.asarray(L0_ffn_b2, dtype=f)
    idn = np.eye(128, dtype=f)
    dmask = np.triu(np.ones((128, 128), dtype=f))

    if _NC_CACHE is None:
        _NC_CACHE = _build_nc()
    nc = _NC_CACHE

    in_maps = []
    for c in range(8):
        in_maps.append({
            "x": np.ascontiguousarray(x[2 * c:2 * c + 2]),
            "wqt": wqt, "wkt": wkt, "wvt": wvt, "wot": wot,
            "bo": bo, "betas": betas, "memz": memz,
            "w1": np.ascontiguousarray(w1[c]),
            "w2t": np.ascontiguousarray(w2[c].T),
            "b1": np.ascontiguousarray(b1[c]),
            "b2": np.ascontiguousarray(b2[c]),
            "idn": idn, "dmask": dmask,
        })
    res = run_bass_kernel_spmd(nc, in_maps, core_ids=list(range(8)))
    out = np.concatenate([np.asarray(r["y"]) for r in res.results], axis=0)
    return out.astype(np.float32)
